# version 23 (final)
# baseline (speedup 1.0000x reference)
import numpy as np
import ml_dtypes
from contextlib import ExitStack
import concourse.bass as bass
import concourse.mybir as mybir
from concourse.bass_utils import run_bass_kernel_spmd

F32 = mybir.dt.float32
BF16 = mybir.dt.bfloat16
U8 = mybir.dt.uint8
AF = mybir.ActivationFunctionType
ALU = mybir.AluOpType
AX = mybir.AxisListType

B, T, C, HQ, HKV, D, FF = 2, 4096, 2048, 16, 4, 128, 8192
CK = C // 128
OWN = 1024
NQ = 4
EPS_LN = 1e-5
EPS_QK = float(np.float32(1e-6))
EPS_QK_128 = float(np.float32(1e-6) * 128.0)

DEBUG_TAPS = False


class Prog:
    ENGINES = ("sync", "scalar", "vector", "tensor", "gpsimd")

    def __init__(self):
        self.ops = {e: [] for e in self.ENGINES}
        self.cnt = {}
        self.rdy = {}
        self.rel = {}

    def v(self, sem):
        return self.cnt.get(sem, 0)

    def emit(self, eng, fn, wait=(), inc=()):
        wl = []
        seen = {}
        for s, n in wait:
            if n > 0 and seen.get(s, -1) < n:
                seen[s] = n
        for s, n in seen.items():
            wl.append((s, n))
        for s, n in inc:
            self.cnt[s] = self.cnt.get(s, 0) + n
        self.ops[eng].append((fn, tuple(wl), tuple(inc)))

    def mark(self, table, key, sem):
        table[key] = [(sem, self.v(sem))]

    def ready(self, key, sem):
        self.mark(self.rdy, key, sem)

    def freed(self, key, sem):
        self.mark(self.rel, key, sem)

    def R(self, *keys):
        out = []
        for k in keys:
            out += self.rdy.get(k, [])
        return out

    def F(self, *keys):
        out = []
        for k in keys:
            out += self.rel.get(k, [])
        return out

    def sems_used(self):
        names = set(self.cnt)
        for e in self.ENGINES:
            for _, waits, _ in self.ops[e]:
                for s, _ in waits:
                    names.add(s)
        return sorted(names)


def build_nc():
    nc = bass.Bass("TRN2", target_bir_lowering=False,
                   detect_race_conditions=False)

    kind_scr = "ExternalOutput" if DEBUG_TAPS else "Internal"
    xb = nc.dram_tensor("xb", [T, C], F32, kind="ExternalInput").ap()
    wq = nc.dram_tensor("wq", [C, HQ * D], F32, kind="ExternalInput").ap()
    wk = nc.dram_tensor("wk", [C, HKV * D], F32, kind="ExternalInput").ap()
    wv = nc.dram_tensor("wv", [C, HKV * D], F32, kind="ExternalInput").ap()
    wo = nc.dram_tensor("wo", [C, C], F32, kind="ExternalInput").ap()
    w1 = nc.dram_tensor("w1", [FF, C], F32, kind="ExternalInput").ap()
    w2 = nc.dram_tensor("w2", [C, FF], F32, kind="ExternalInput").ap()
    b1c = nc.dram_tensor("b1c", [128, FF // 128], F32, kind="ExternalInput").ap()
    b2c = nc.dram_tensor("b2c", [128, C // 128], F32, kind="ExternalInput").ap()
    identf_d = nc.dram_tensor("identf", [128, 128], F32, kind="ExternalInput").ap()
    onescb_d = nc.dram_tensor("onescb", [128, 1], BF16, kind="ExternalInput").ap()
    onesrb_d = nc.dram_tensor("onesrb", [1, 128], BF16, kind="ExternalInput").ap()
    onescf_d = nc.dram_tensor("onescf", [128, 1], F32, kind="ExternalInput").ap()
    onesrf_d = nc.dram_tensor("onesrf", [1, 128], F32, kind="ExternalInput").ap()
    epsc_d = nc.dram_tensor("epsc", [128, 4], F32, kind="ExternalInput").ap()
    out_d = nc.dram_tensor("out", [OWN, C], F32, kind="ExternalOutput").ap()

    xnsc = nc.dram_tensor("xnsc", [T, C], BF16, kind=kind_scr).ap()
    w1qd = nc.dram_tensor("w1qd", [FF, C], BF16, kind=kind_scr).ap()
    w2qd = nc.dram_tensor("w2qd", [C, FF], BF16, kind=kind_scr).ap()
    ssc = nc.dram_tensor("ssc", [C, OWN], BF16, kind=kind_scr).ap()

    ctx = ExitStack()
    ARENA_BYTES = 204800
    arena = nc.alloc_sbuf_tensor("arena", [128, ARENA_BYTES], U8)
    base = nc.lookup_mloc(arena).addr

    def sub(name, shape, dtype, off):
        nbytes = int(np.prod(shape[1:])) * mybir.dt.size(dtype)
        assert off % 32 == 0, (name, off)
        assert off + nbytes <= ARENA_BYTES, (name, off, nbytes)
        return nc.alloc_sbuf_tensor_at(name, shape, dtype, offset=base + off)

    A0, B0, C0, DE, FB = 0, 65536, 98304, 131072, 188416
    kbuf = sub("kbuf", [128, 16384], BF16, A0)
    vbuf = sub("vbuf", [128, 16384], BF16, A0 + 32768)
    wqb = sub("wqb", [128, 32768], BF16, A0)
    wob = sub("wob", [128, 32768], BF16, A0)
    hbuf = sub("hbuf", [128, 16384], BF16, A0)
    acch = sub("acch", [128, 8192], F32, A0 + 32768)
    xoring = sub("xoring", [128, 4096], F32, A0)
    qbuf = sub("qbuf", [128, 16384], BF16, B0)
    accl = sub("accl", [128, 8192], F32, B0)
    xnT = sub("xnT", [128, 16384], BF16, C0)
    abuf = sub("abuf", [128, 16384], BF16, C0)
    xr = [sub(f"xr{i}", [128, 2048], F32, DE + 8192 * i) for i in range(2)]
    xnr = [sub(f"xnr{i}", [128, 2048], BF16, DE + 16384 + 4096 * i) for i in range(2)]
    wqr = [sub(f"wqr{i}", [128, 2048], F32, DE + 24576 + 8192 * i) for i in range(2)]
    wkvb = sub("wkvb", [128, 8192], BF16, DE + 24576)
    wkvr = [sub(f"wkvr{i}", [128, 512], F32, DE + 40960 + 2048 * i) for i in range(2)]
    ws1r = [sub(f"ws1r{i}", [128, 512], F32, DE + 45056 + 2048 * i) for i in range(2)]
    ws2r = [sub(f"ws2r{i}", [128, 512], F32, DE + 49152 + 2048 * i) for i in range(2)]
    qwf = [sub(f"qwf{i}", [128, 2048], F32, DE + 8192 * i) for i in range(2)]
    qc2 = sub("qc2", [128, 2048], BF16, DE + 16384)
    qqt = [sub(f"qqt{i}", [128, 2048], BF16, DE + 20480 + 4096 * i) for i in range(2)]
    ptile = sub("ptile", [128, 4096], BF16, DE + 28672)
    rzbb = [sub(f"rzbb{i}", [128, 512], BF16, DE + 36864 + 1024 * i) for i in range(2)]
    wor = [sub(f"wor{i}", [128, 2048], F32, DE + 8192 * i) for i in range(2)]
    stv = [sub(f"stv{i}", [128, 512], BF16, DE + 16384 + 1024 * i) for i in range(2)]
    m1r = [sub(f"m1r{i}", [128, 8192], BF16, DE + 16384 * i) for i in range(2)]
    m2r = [sub(f"m2r{i}", [128, 4096], BF16, DE + 32768 + 8192 * i) for i in range(3)]
    stre = [sub(f"stre{i}", [128, 1024], BF16, DE + 49152 + 2048 * i) for i in range(2)]
    gtmp = sub("gtmp", [128, 1024], F32, DE + 53248)
    identf = sub("identf_s", [128, 128], F32, FB + 0)
    onescb = sub("onescb_s", [128, 1], BF16, FB + 512)
    onesrb = sub("onesrb_s", [1, 128], BF16, FB + 544)
    onescf = sub("onescf_s", [128, 1], F32, FB + 800)
    onesrf = sub("onesrf_s", [1, 128], F32, FB + 832)
    b1s = sub("b1s", [128, 64], F32, FB + 1344)
    b2s = sub("b2s", [128, 16], F32, FB + 1600)
    lnS = {k: sub(f"ln_{k}", [128, 8], F32, FB + 1664 + 32 * i)
           for i, k in enumerate(["sm", "sq", "m", "e", "v", "r", "nm", "t"])}
    sdump = sub("sdump", [128, 1024], BF16, FB + 1952)
    praw = [sub(f"praw{i}", [128, 512], BF16, FB + 4000 + 1024 * i) for i in range(2)]
    praw.append(sub("praw2", [128, 512], BF16, FB + 7072))
    ksq = sub("ksq", [128, 512], BF16, FB + 6048)
    rbs = [sub("rb0", [1, 512], BF16, FB + 8096),
           sub("rb1", [1, 512], BF16, FB + 9120)]
    rb = rbs[0]
    sd = sub("sd", [1, 512], F32, FB + 13440)
    w1p = sub("w1p", [128, 256], F32, FB + 10144)
    w2p = sub("w2p", [128, 256], F32, FB + 11168)
    hs1b = sub("hs1b", [128, 1], F32, FB + 12192)
    hs1n = sub("hs1n", [128, 1], F32, FB + 12224)
    hs2b = sub("hs2b", [128, 1], F32, FB + 12256)
    hs2n = sub("hs2n", [128, 1], F32, FB + 12288)
    wtot = sub("wtot", [128, 1], F32, FB + 12320)
    hst = sub("hst", [1, 1], F32, FB + 12352)
    epss = sub("epss", [128, 4], F32, FB + 12384)
    spacebuf = sub("spacebuf", [128, 512], F32, DE + 53248)
    azb = sub("azb", [128, 512], BF16, FB + 12416)

    psA = ctx.enter_context(nc.psum_tensor([128, 2048], F32))
    psB = ctx.enter_context(nc.psum_tensor([128, 2048], F32))

    def bank(i):
        t = psA if i < 4 else psB
        j = i % 4
        return t[:, j * 512:(j + 1) * 512]

    p = Prog()

    p.emit("vector", (lambda e: e.memset(spacebuf[:, :], 0.0)))

    def vsp():
        p.emit("vector", (lambda e: e.tensor_copy(out=spacebuf[:, :],
                                                  in_=spacebuf[:, :])))

    for nm, dst, src in [("identf", identf, identf_d), ("onescb", onescb, onescb_d),
                         ("onesrb", onesrb, onesrb_d), ("onescf", onescf, onescf_d),
                         ("onesrf", onesrf, onesrf_d), ("b1s", b1s, b1c),
                         ("b2s", b2s, b2c), ("epss", epss, epsc_d)]:
        p.emit("sync", (lambda e, d=dst, s=src: e.dma_start(out=d[:, :], in_=s)),
               inc=(("cst", 16),))
    p.ready("cst", "cst")
    CSTW = p.R("cst")

    w1sl = [(w1[128 * r:128 * (r + 1), 512 * cb:512 * (cb + 1)], 4 * r + cb)
            for r in range(64) for cb in range(4)]
    w2sl = [(w2[128 * r:128 * (r + 1), 512 * fb:512 * (fb + 1)], 16 * r + fb)
            for r in range(16) for fb in range(16)]
    sc = {"i1": 0, "i2": 0}

    def emit_scale(n):
        for _ in range(n):
            i = sc["i1"]
            if i < 256:
                sc["i1"] += 1
                src, col = w1sl[i]
                slot = i % 2
                p.emit("sync",
                       (lambda e, d=ws1r[slot], s=src: e.dma_start(out=d[:, :], in_=s)),
                       wait=CSTW + p.F(("ws1", slot)), inc=((f"s1l{slot}", 16),))
                p.ready(("ws1d", slot), f"s1l{slot}")
                p.emit("scalar", (lambda e, sl=slot, cl=col: e.activation(
                    out=sdump[:, 0:512], in_=ws1r[sl][:, :], func=AF.Abs,
                    accum_out=w1p[:, cl:cl + 1])),
                    wait=p.R(("ws1d", slot)), inc=(("w1abs", 1),))
                p.freed(("ws1", slot), "w1abs")
            j = sc["i2"]
            if j < 256:
                sc["i2"] += 1
                src, col = w2sl[j]
                slot = j % 2
                p.emit("sync",
                       (lambda e, d=ws2r[slot], s=src: e.dma_start(out=d[:, :], in_=s)),
                       wait=CSTW + p.F(("ws2", slot)), inc=((f"s2l{slot}", 16),))
                p.ready(("ws2d", slot), f"s2l{slot}")
                p.emit("vector", (lambda e, sl=slot, cl=col: e.tensor_reduce(
                    out=w2p[:, cl:cl + 1], in_=ws2r[sl][:, :], axis=AX.X,
                    op=ALU.add, apply_absolute_value=True)),
                    wait=p.R(("ws2d", slot)), inc=(("w2abs", 1),))
                p.freed(("ws2", slot), "w2abs")

    proj_pend = {"B": [], "D": []}

    def proj_flush(nB=99, nD=99):
        for _ in range(min(nD, len(proj_pend["D"]))):
            proj_pend["D"].pop(0)()
        for _ in range(min(nB, len(proj_pend["B"]))):
            proj_pend["B"].pop(0)()
        if nD >= 99:
            while proj_pend["D"]:
                proj_pend["D"].pop(0)()

    def emit_proj_norm(gp, bk, dst, dst_off, is_q):
        slot = gp % 3
        rslot = gp % 2
        p.emit("scalar", (lambda e, b=bk: e.activation(
            out=ksq[:, :], in_=bank(b), func=AF.Square)),
            wait=p.R(("bk", bk)) + p.F("ksq"), inc=(("psq", 1),))
        psq_now = [("psq", p.v("psq"))]
        p.emit("scalar", (lambda e, b=bk, sl=slot: e.copy(
            out=praw[sl][:, :], in_=bank(b))),
            wait=p.F(("praw", slot)), inc=(("prw", 1),))
        p.freed(("bk", bk), "prw")
        prw_now = [("prw", p.v("prw"))]

        def stageBC():
            ssb = 2 + gp % 2
            p.emit("tensor", (lambda e, b=ssb: e.matmul(
                bank(b)[0:1, :], onescb[:, 0:1], ksq[:, :],
                start=True, stop=True)),
                wait=psq_now + p.F(("bk", ssb)), inc=(("pss", 1),))
            p.freed("ksq", "pss")
            pss_now = [("pss", p.v("pss"))]
            if is_q:
                p.emit("scalar", (lambda e, b=ssb: e.activation(
                    out=sd[0:1, :], in_=bank(b)[0:1, :], func=AF.Sqrt,
                    bias=epss[0:1, 2:3])),
                    wait=pss_now + p.F("sd"), inc=(("psd", 1),))
            else:
                p.emit("scalar", (lambda e, b=ssb: e.activation(
                    out=sd[0:1, :], in_=bank(b)[0:1, :], func=AF.Sqrt,
                    bias=epss[0:1, 1:2], scale=1.0 / 128)),
                    wait=pss_now + p.F("sd"), inc=(("psd", 1),))
            p.freed(("bk", ssb), "psd")
            psd_now = [("psd", p.v("psd"))]

            def _recip_rb(e, sl=rslot):
                with nc.allow_low_precision(reason="1/x rounded once to bf16"):
                    return e.reciprocal(out=rbs[sl][0:1, :], in_=sd[0:1, :])
            p.emit("vector", _recip_rb,
                   wait=psd_now + p.F(("rb", rslot)), inc=(("prb", 1),))
            p.freed("sd", "prb")
            prb_now = [("prb", p.v("prb"))]

            def stageDE():
                bcb = 4 + gp % 2
                p.emit("tensor", (lambda e, b=bcb, sl=rslot: e.matmul(
                    bank(b), onesrb[0:1, :], rbs[sl][0:1, :],
                    start=True, stop=True)),
                    wait=prb_now + p.F(("bk", bcb)), inc=(("pbc", 1),))
                p.freed(("rb", rslot), "pbc")
                p.emit("vector", (lambda e, b=bcb, sl=slot, d=dst, o=dst_off:
                                  e.tensor_tensor(out=d[:, o:o + 512],
                                                  in0=praw[sl][:, :],
                                                  in1=bank(b), op=ALU.mult)),
                       wait=[("pbc", p.v("pbc"))] + prw_now,
                       inc=(("ptt", 1),))
                p.freed(("praw", slot), "ptt")
                p.freed(("bk", bcb), "ptt")
            proj_pend["D"].append(stageDE)

        proj_pend["B"].append(stageBC)

    def ln_steps(q):
        qo = q * OWN
        steps = []

        def tileA(lt):
            def go():
                tt = q * 8 + lt
                slot = tt % 2
                p.emit("scalar", (lambda e, d=xr[slot],
                                  sr=xb[qo + 128 * lt:qo + 128 * (lt + 1), :]:
                                  e.dma_start(out=d[:, :], in_=sr)),
                       wait=p.F(("xr", slot)), inc=((f"xl{slot}", 16),))
                p.ready(("xrd", slot), f"xl{slot}")
                xw = p.R(("xrd", slot))
                p.emit("scalar", (lambda e, sl=slot, c=lt: e.activation(
                    out=sdump[:, :], in_=xr[sl][:, 0:1024], func=AF.Square,
                    accum_out=lnS["t"][:, c:c + 1])), wait=xw + p.F("lnrowA"))
                p.emit("scalar", (lambda e, sl=slot, c=lt: e.activation(
                    out=sdump[:, :], in_=xr[sl][:, 1024:2048], func=AF.Square,
                    accum_out=lnS["sq"][:, c:c + 1])), inc=(("lnsq", 1),))
                p.emit("vector", (lambda e, sl=slot, c=lt: e.tensor_reduce(
                    out=lnS["sm"][:, c:c + 1], in_=xr[sl][:, :], axis=AX.X,
                    op=ALU.add)), wait=xw + p.F("lnrowA"), inc=(("lnsm", 1),))
                p.rel[("xr", slot)] = [("lnsq", p.v("lnsq")), ("lnsm", p.v("lnsm"))]
            return go

        for lt in range(8):
            steps.append(tileA(lt))

        def stats():
            w0 = [("lnsq", p.v("lnsq")), ("lnsm", p.v("lnsm"))] + p.F("lnrowB")
            vsp()
            p.emit("vector", (lambda e: e.tensor_add(
                out=lnS["sq"][:, :], in0=lnS["sq"][:, :], in1=lnS["t"][:, :])),
                wait=w0)
            p.emit("vector", (lambda e: e.tensor_scalar_mul(
                out=lnS["m"][:, :], in0=lnS["sm"][:, :], scalar1=1.0 / 2048)))
            vsp()
            p.emit("vector", (lambda e: e.tensor_scalar_mul(
                out=lnS["e"][:, :], in0=lnS["sq"][:, :], scalar1=1.0 / 2048)))
            vsp()
            p.emit("vector", (lambda e: e.tensor_mul(
                out=lnS["t"][:, :], in0=lnS["m"][:, :], in1=lnS["m"][:, :])))
            vsp()
            p.emit("vector", (lambda e: e.tensor_sub(
                out=lnS["v"][:, :], in0=lnS["e"][:, :], in1=lnS["t"][:, :])),
                inc=(("lnv", 1),))
            p.emit("scalar", (lambda e: e.activation(
                out=lnS["v"][:, :], in_=lnS["v"][:, :], func=AF.Sqrt,
                bias=epss[:, 0:1])), wait=[("lnv", p.v("lnv"))],
                inc=(("lnsd", 1),))
            p.emit("vector", (lambda e: e.reciprocal(
                out=lnS["r"][:, :], in_=lnS["v"][:, :])),
                wait=[("lnsd", p.v("lnsd"))])
            vsp()
            p.emit("vector", (lambda e: e.tensor_mul(
                out=lnS["nm"][:, :], in0=lnS["m"][:, :], in1=lnS["r"][:, :])))
            vsp()
            p.emit("vector", (lambda e: e.tensor_scalar_mul(
                out=lnS["nm"][:, :], in0=lnS["nm"][:, :], scalar1=-1.0)),
                inc=(("lnst", 1),))
            p.freed("lnrowA", "lnst")

        steps.append(stats)

        def tileB(lt):
            def go():
                LNST = [("lnst", p.v("lnst"))]
                slot = lt % 2
                p.emit("scalar", (lambda e, d=xr[slot],
                                  sr=xb[qo + 128 * lt:qo + 128 * (lt + 1), :]:
                                  e.dma_start(out=d[:, :], in_=sr)),
                       wait=p.F(("xr", slot)), inc=((f"xl{slot}", 16),))
                p.ready(("xrd", slot), f"xl{slot}")
                p.emit("scalar", (lambda e, sl=slot, c=lt: e.activation(
                    out=xnr[sl][:, :], in_=xr[sl][:, :], func=AF.Identity,
                    scale=lnS["r"][:, c:c + 1], bias=lnS["nm"][:, c:c + 1])),
                    wait=LNST + p.R(("xrd", slot)) + p.F(("xnr", slot)),
                    inc=(("lnap", 1),))
                p.freed(("xr", slot), "lnap")
                p.emit("scalar", (lambda e, sl=slot,
                                  d=xnsc[qo + 128 * lt:qo + 128 * (lt + 1), :]:
                                  e.dma_start(out=d, in_=xnr[sl][:, :])),
                       wait=[("lnap", p.v("lnap"))], inc=(("xnw", 16),))
                p.freed(("xnr", slot), "xnw")
                if lt == 7:
                    p.freed("lnrowB", "lnap")
                    p.ready(("lnq", q), "xnw")
            return go

        for lt in range(8):
            steps.append(tileB(lt))
        return steps

    gp_counter = {"gp": 0}

    filler = {"steps": []}

    sc_budget = {"n": 130}

    def fill(n=1):
        for _ in range(n):
            if filler["steps"]:
                filler["steps"].pop(0)()
            elif sc_budget["n"] > 0:
                sc_budget["n"] -= 1
                emit_scale(1)

    def drain_filler():
        while filler["steps"]:
            filler["steps"].pop(0)()

    def p1_quarter(q):
        qo = q * OWN
        if q == 0:
            for ck in range(CK):
                slot = ck % 2
                p.emit("scalar", (lambda e, d=wqr[slot],
                                  sr=wq[128 * ck:128 * (ck + 1), :]:
                                  e.dma_start(out=d[:, :], in_=sr)),
                       wait=CSTW + p.F(("wqr", slot)), inc=((f"wql{slot}", 16),))
                p.ready(("wqrd", slot), f"wql{slot}")
                p.emit("vector", (lambda e, sl=slot, c=ck: e.tensor_copy(
                    out=wqb[:, 2048 * c:2048 * (c + 1)], in_=wqr[sl][:, :])),
                    wait=p.R(("wqrd", slot)), inc=(("wqc", 1),))
                p.freed(("wqr", slot), "wqc")
            p.ready("wqb", "wqc")
            for st in ln_steps(0):
                st()
        else:
            drain_filler()
        if q + 1 < NQ:
            filler["steps"] = ln_steps(q + 1)

        for ck in range(CK):
            p.emit("sync", (lambda e, c=ck,
                            sr=xnsc[qo:qo + OWN, 128 * ck:128 * (ck + 1)]:
                            e.dma_start_transpose(
                                xnT[:, 1024 * c:1024 * (c + 1)], sr)),
                   wait=p.R(("lnq", q)) + p.F(("xnT", ck)),
                   inc=(("xta", 16),))
        p.ready("xnTall", "xta")

        if q == 0:
            for pr in range(HQ * 2):
                h, jt = pr // 2, pr % 2
                bk = pr % 2
                for ck in range(CK):
                    w = []
                    if ck == 0:
                        w += p.R("xnTall", "wqb") + p.F(("bk", bk))
                    p.emit("tensor", (lambda e, b=bk, c=ck, hh=h, j=jt:
                                      e.matmul(bank(b),
                                               wqb[:, 2048 * c + 128 * hh:2048 * c + 128 * (hh + 1)],
                                               xnT[:, 1024 * c + 512 * j:1024 * c + 512 * (j + 1)],
                                               start=(c == 0), stop=(c == CK - 1))),
                           wait=w, inc=((("pmm", 1),) if ck == CK - 1 else ()))
                p.ready(("bk", bk), "pmm")
                proj_flush(nB=1, nD=1 if len(proj_pend["D"]) >= 2 else 0)
                emit_proj_norm(gp_counter["gp"], bk, qbuf, h * 1024 + jt * 512, True)
                gp_counter["gp"] += 1
            p.ready("qdone", "ptt")

        for ck in range(CK):
            slot = ck % 2
            p.emit("scalar", (lambda e, d=wkvr[slot],
                              sr=wk[128 * ck:128 * (ck + 1), :]:
                              e.dma_start(out=d[:, :], in_=sr)),
                   wait=CSTW + p.F(("wkvr", slot)), inc=((f"wvl{slot}", 16),))
            p.ready(("wkvrd", slot), f"wvl{slot}")
            w = p.R(("wkvrd", slot)) + p.F(("wkvb", ck))
            if q == 0:
                w += [("wqc", 16)]
            p.emit("vector", (lambda e, sl=slot, c=ck: e.tensor_copy(
                out=wkvb[:, 512 * c:512 * (c + 1)], in_=wkvr[sl][:, :])),
                wait=w, inc=(("wkc", 1),))
            p.freed(("wkvr", slot), "wkc")
        for pr in range(HKV * 2):
            kv, jt = pr // 2, pr % 2
            bk = pr % 2
            WKB = [("wkc", p.v("wkc"))]
            for ck in range(CK):
                w = []
                if ck == 0:
                    w = p.R("xnTall") + WKB + p.F(("bk", bk))
                p.emit("tensor", (lambda e, b=bk, c=ck, k=kv, j=jt:
                                  e.matmul(bank(b),
                                           wkvb[:, 512 * c + 128 * k:512 * c + 128 * (k + 1)],
                                           xnT[:, 1024 * c + 512 * j:1024 * c + 512 * (j + 1)],
                                           start=(c == 0), stop=(c == CK - 1))),
                       wait=w, inc=((("pmm", 1),) if ck == CK - 1 else ()))
                if ck % 2 == 1:
                    fill(1)
            p.ready(("bk", bk), "pmm")
            proj_flush(nB=1, nD=1 if len(proj_pend["D"]) >= 2 else 0)
            emit_proj_norm(gp_counter["gp"], bk,
                           kbuf, kv * 4096 + q * 1024 + jt * 512, False)
            gp_counter["gp"] += 1
        KMM_END = p.v("pmm")

        proj_flush(nB=9, nD=len(proj_pend["D"]))
        for ck in range(CK):
            slot = ck % 2
            p.emit("scalar", (lambda e, d=wkvr[slot],
                              sr=wv[128 * ck:128 * (ck + 1), :]:
                              e.dma_start(out=d[:, :], in_=sr)),
                   wait=p.F(("wkvr", slot)), inc=((f"wvl{slot}", 16),))
            p.ready(("wkvrd", slot), f"wvl{slot}")
            p.emit("vector", (lambda e, sl=slot, c=ck: e.tensor_copy(
                out=wkvb[:, 512 * c:512 * (c + 1)], in_=wkvr[sl][:, :])),
                wait=p.R(("wkvrd", slot)) + [("pmm", KMM_END)],
                inc=(("wkc", 1),))
            p.freed(("wkvr", slot), "wkc")
        for kc in range(8):
            bk = kc % 2
            WKB = [("wkc", p.v("wkc"))]
            for ck in range(CK):
                w = []
                if ck == 0:
                    w = p.R("xnTall") + WKB + p.F(("bk", bk))
                p.emit("tensor", (lambda e, b=bk, c=ck, k=kc:
                                  e.matmul(bank(b),
                                           xnT[:, 1024 * c + 128 * k:1024 * c + 128 * (k + 1)],
                                           wkvb[:, 512 * c:512 * (c + 1)],
                                           start=(c == 0), stop=(c == CK - 1))),
                       wait=w, inc=((("pmm", 1),) if ck == CK - 1 else ()))
                if ck % 2 == 1:
                    fill(1)
            p.ready(("bk", bk), "pmm")
            kcg = q * 8 + kc
            p.emit("scalar", (lambda e, b=bk, o=512 * kcg: e.copy(
                out=vbuf[:, o:o + 512], in_=bank(b))),
                wait=p.R(("bk", bk)), inc=(("vev", 1),))
            p.freed(("bk", bk), "vev")
        proj_flush()
        VMM_END = p.v("pmm")
        for ck in range(CK):
            p.rel[("xnT", ck)] = [("pmm", VMM_END)]
            p.rel[("wkvb", ck)] = [("pmm", VMM_END)]

    for q in range(NQ):
        p1_quarter(q)
    drain_filler()
    proj_flush()

    KV_READY = [("ptt", p.v("ptt")), ("vev", p.v("vev"))]

    NCH = 32

    SBANKS = (0, 1, 4, 5)

    def emit_pair(pr):
        h, jt = pr // 2, pr % 2
        kv = h // 4
        qoff = h * 1024 + jt * 512
        pvb = 2 + pr % 2
        zbk = 6 + pr % 2

        def emit_S(c):
            gc = pr * NCH + c
            sb = SBANKS[gc % 4]
            w = p.F(("bk", sb))
            if c == 0:
                w += KV_READY + p.F(("bk", pvb), ("bk", zbk))
                if pr >= 6:
                    w += [("hsx", 4)]
            p.emit("tensor", (lambda e, b=sb, k=kv, cc=c, qo=qoff:
                              e.matmul(bank(b),
                                       kbuf[:, k * 4096 + 128 * cc:k * 4096 + 128 * (cc + 1)],
                                       qbuf[:, qo:qo + 512],
                                       start=True, stop=True)),
                   wait=w, inc=(("aS", 1),))
            p.ready(("bk", sb), "aS")
            pslot = gc % 8
            p.emit("scalar", (lambda e, b=sb, sl=pslot: e.activation(
                out=ptile[:, 512 * sl:512 * (sl + 1)], in_=bank(b), func=AF.Exp)),
                wait=p.R(("bk", sb)) + p.F(("pt", pslot)), inc=(("aE", 1),))
            p.freed(("bk", sb), "aE")
            aE_now = [("aE", p.v("aE"))]
            if c == 0:
                p.emit("vector", (lambda e, sl=pslot: e.tensor_copy(
                    out=azb[:, :], in_=ptile[:, 512 * sl:512 * (sl + 1)])),
                    wait=aE_now + p.F("azb"), inc=(("aZv", 1),))
            else:
                p.emit("vector", (lambda e, sl=pslot: e.tensor_add(
                    out=azb[:, :], in0=azb[:, :],
                    in1=ptile[:, 512 * sl:512 * (sl + 1)])),
                    wait=aE_now, inc=(("aZv", 1),))

        def emit_PV(c):
            gc = pr * NCH + c
            pslot = gc % 8
            p.emit("tensor", (lambda e, b=pvb, sl=pslot, cc=c, k=kv: e.matmul(
                bank(b), vbuf[:, 512 * cc + 128 * k:512 * cc + 128 * (k + 1)],
                ptile[:, 512 * sl:512 * (sl + 1)],
                start=(cc == 0), stop=(cc == NCH - 1))),
                wait=[("aE", ae_mark[c])], inc=(("aPV", 1),))
            p.rel[("pt", pslot)] = [("aPV", p.v("aPV")), ("aZv", azv_mark[c])]

        ae_mark = {}
        azv_mark = {}
        for c in range(NCH):
            emit_S(c)
            ae_mark[c] = p.v("aE")
            azv_mark[c] = p.v("aZv")
            if c >= 4:
                emit_PV(c - 4)
            if c % 2 == 1:
                if pr < 6:
                    emit_scale(3)
                else:
                    emit_quant_ops(1)
        for c in range(NCH - 4, NCH):
            emit_PV(c)
        p.ready(("bk", pvb), "aPV")
        p.ready(("bk", pvb), "aPV")
        p.emit("tensor", (lambda e, b=zbk: e.matmul(
            bank(b)[0:1, :], onescb[:, 0:1], azb[:, :], start=True, stop=True)),
            wait=[("aZv", p.v("aZv"))] + p.F(("bk", zbk)), inc=(("aZ", 1),))
        p.ready(("bk", zbk), "aZ")
        p.freed("azb", "aZ")

        def _recip_rz(e, b=zbk):
            with nc.allow_low_precision(reason="1/Z rounded once to bf16"):
                return e.reciprocal(out=rb[0:1, :], in_=bank(b)[0:1, :])
        p.emit("vector", _recip_rz,
               wait=p.R(("bk", zbk)) + p.F(("rb", 0)), inc=(("aRZ", 1),))
        p.emit("tensor", (lambda e, b=zbk: e.matmul(
            bank(b), onesrb[0:1, :], rb[0:1, :], start=True, stop=True)),
            wait=[("aRZ", p.v("aRZ"))], inc=(("aBC", 1),))
        p.freed(("rb", 0), "aBC")
        p.ready(("bk", zbk), "aBC")
        p.emit("scalar", (lambda e, b=zbk, sl=pr % 2: e.copy(
            out=rzbb[sl][:, :], in_=bank(b))),
            wait=p.R(("bk", zbk)) + p.F(("rzbb", pr % 2)), inc=(("aBB", 1),))
        p.freed(("bk", zbk), "aBB")
        p.emit("vector", (lambda e, b=pvb, sl=pr % 2, qo=qoff: e.tensor_tensor(
            out=abuf[:, qo:qo + 512], in0=bank(b), in1=rzbb[sl][:, :],
            op=ALU.mult)),
            wait=[("aBB", p.v("aBB"))] + p.R(("bk", pvb)), inc=(("aAH", 1),))
        p.freed(("bk", pvb), "aAH")
        p.freed(("rzbb", pr % 2), "aAH")

    def emit_scale_finalize():
        w = [("w1abs", 256), ("w2abs", 256)]
        for which, hb, hn in [(1, hs1b, hs1n), (2, hs2b, hs2n)]:
            wp = w1p if which == 1 else w2p
            p.emit("vector", (lambda e, t=wp: e.tensor_reduce(
                out=wtot[:, 0:1], in_=t[:, :], axis=AX.X, op=ALU.add)),
                wait=w + p.F("wtot"), inc=(("hsv", 1),))
            p.emit("tensor", (lambda e: e.matmul(
                bank(0)[0:1, 0:1], onescf[:, 0:1], wtot[:, 0:1],
                start=True, stop=True)),
                wait=[("hsv", p.v("hsv"))] + p.F(("bk", 0)), inc=(("hsm", 1),))
            p.ready(("bk", 0), "hsm")
            p.emit("scalar", (lambda e: e.activation(
                out=hst[0:1, 0:1], in_=bank(0)[0:1, 0:1], func=AF.Identity,
                scale=0.5 / (FF * C))),
                wait=[("hsm", p.v("hsm"))], inc=(("hss", 1),))
            p.freed(("bk", 0), "hss")
            p.freed("wtot", "hss")
            p.emit("tensor", (lambda e: e.matmul(
                bank(1)[:, 0:1], onesrf[0:1, :], hst[0:1, 0:1],
                start=True, stop=True)),
                wait=[("hss", p.v("hss"))] + p.F(("bk", 1)), inc=(("hsm", 1),))
            p.ready(("bk", 1), "hsm")
            p.emit("scalar", (lambda e, t=hb: e.copy(out=t[:, :], in_=bank(1)[:, 0:1])),
                   wait=[("hsm", p.v("hsm"))], inc=(("hsx", 1),))
            p.freed(("bk", 1), "hsx")
            p.emit("vector", (lambda e, ti=hn, t=hb: e.tensor_scalar_mul(
                out=ti[:, :], in0=t[:, :], scalar1=-1.0)),
                wait=[("hsx", p.v("hsx"))], inc=(("hsx", 1),))
        p.ready("hs", "hsx")

    q1sl = [(w1[128 * r:128 * (r + 1), :], w1qd[128 * r:128 * (r + 1), :])
            for r in range(64)]
    q2sl = [(w2[128 * r:128 * (r + 1), 2048 * fb:2048 * (fb + 1)],
             w2qd[128 * r:128 * (r + 1), 2048 * fb:2048 * (fb + 1)])
            for r in range(16) for fb in range(4)]
    qn = {"i": 0, "ph": 0}

    def emit_quant_ops(k):
        for _ in range(k):
            i = qn["i"]
            if i >= 128:
                return
            ph = qn["ph"]
            src, dst = q1sl[i] if i < 64 else q2sl[i - 64]
            hsb = hs1b if i < 64 else hs2b
            hsn = hs1n if i < 64 else hs2n
            wsem = "w1qw" if i < 64 else "w2qw"
            slot = i % 2
            if ph == 0:
                p.emit("scalar", (lambda e, d=qwf[slot], sr=src: e.dma_start(
                    out=d[:, :], in_=sr)),
                    wait=p.R("hs") + p.F(("qwf", slot)), inc=((f"qwl{slot}", 16),))
                p.ready(("qwfd", slot), f"qwl{slot}")
                p.emit("vector", (lambda e, sl=slot, hn=hsn: e.tensor_single_scalar(
                    out=qc2[:, 0:1024], in_=qwf[sl][:, 0:1024],
                    scalar=hn[:, 0:1], op=ALU.is_lt)),
                    wait=p.R(("qwfd", slot), "hs") + p.F(("qqt", slot)))
            elif ph == 1:
                p.emit("vector", (lambda e, sl=slot, hb=hsb: e.scalar_tensor_tensor(
                    out=qqt[sl][:, 0:1024], in0=qwf[sl][:, 0:1024],
                    scalar=hb[:, 0:1], in1=qc2[:, 0:1024],
                    op0=ALU.is_gt, op1=ALU.subtract)))
            elif ph == 2:
                p.emit("vector", (lambda e, sl=slot, hn=hsn: e.tensor_single_scalar(
                    out=qc2[:, 1024:2048], in_=qwf[sl][:, 1024:2048],
                    scalar=hn[:, 0:1], op=ALU.is_lt)))
            else:
                p.emit("vector", (lambda e, sl=slot, hb=hsb: e.scalar_tensor_tensor(
                    out=qqt[sl][:, 1024:2048], in0=qwf[sl][:, 1024:2048],
                    scalar=hb[:, 0:1], in1=qc2[:, 1024:2048],
                    op0=ALU.is_gt, op1=ALU.subtract)),
                    inc=(("qst", 1),))
                p.freed(("qwf", slot), "qst")
                p.emit("scalar", (lambda e, sl=slot, d=dst: e.dma_start(
                    out=d, in_=qqt[sl][:, :])),
                    wait=[("qst", p.v("qst"))], inc=((wsem, 16),))
                p.freed(("qqt", slot), wsem)
            qn["ph"] = (ph + 1) % 4
            if qn["ph"] == 0:
                qn["i"] += 1

    for pr in range(2 * HQ):
        emit_pair(pr)
        if pr == 5:
            emit_scale(520)
            assert sc["i1"] == 256 and sc["i2"] == 256, sc
            emit_scale_finalize()
        if pr >= 6:
            emit_quant_ops(4)
    emit_quant_ops(4 * 128)
    assert qn["i"] == 128, qn

    AH_ALL = [("aAH", p.v("aAH"))]
    for hd in range(CK):
        slot = hd % 2
        p.emit("scalar", (lambda e, d=wor[slot],
                          s=wo[128 * hd:128 * (hd + 1), :]: e.dma_start(out=d[:, :], in_=s)),
               wait=[("qst", 128)] + p.F(("wor", slot)), inc=((f"wol{slot}", 16),))
        p.ready(("word", slot), f"wol{slot}")
        p.emit("vector", (lambda e, s=slot, c=hd: e.tensor_copy(
            out=wob[:, 2048 * c:2048 * (c + 1)], in_=wor[s][:, :])),
            wait=p.R(("word", slot)) + AH_ALL, inc=(("woc", 1),))
        p.freed(("wor", slot), "woc")
        p.ready(("wobd", hd), "woc")
    for pw in range(CK * 2):
        ck, jt = pw // 2, pw % 2
        bk = pw % 2
        for hd in range(CK):
            w = p.R(("wobd", hd))
            if hd == 0:
                w += AH_ALL + p.F(("bk", bk))
            p.emit("tensor", (lambda e, b=bk, d=hd, c=ck, j=jt: e.matmul(
                bank(b), wob[:, 2048 * d + 128 * c:2048 * d + 128 * (c + 1)],
                abuf[:, 1024 * d + 512 * j:1024 * d + 512 * (j + 1)],
                start=(d == 0), stop=(d == CK - 1))),
                wait=w, inc=((("smm", 1),) if hd == CK - 1 else ()))
        p.ready(("bk", bk), "smm")
        p.emit("scalar", (lambda e, b=bk, s=pw % 2: e.copy(
            out=stv[s][:, :], in_=bank(b))),
            wait=p.R(("bk", bk)) + p.F(("stv", pw % 2)), inc=(("sevc", 1),))
        p.freed(("bk", bk), "sevc")
        p.emit("scalar", (lambda e, s=pw % 2,
                          d=ssc[128 * ck:128 * (ck + 1), 512 * jt:512 * (jt + 1)]:
                          e.dma_start(out=d, in_=stv[s][:, :])),
               wait=[("sevc", p.v("sevc"))], inc=(("sev", 16),))
        p.freed(("stv", pw % 2), "sev")
    SMM_END = p.v("smm")
    SEV_ALL = [("sev", p.v("sev"))]

    for ck in range(CK):
        p.emit("sync", (lambda e, c=ck, s=xnsc[0:OWN, 128 * ck:128 * (ck + 1)]:
                        e.dma_start_transpose(xnT[:, 1024 * c:1024 * (c + 1)], s)),
               wait=[("smm", SMM_END)], inc=(("xta", 16),))
    p.ready("xnTall", "xta")

    W1QW = [("w1qw", p.v("w1qw"))]
    W2QW = [("w2qw", p.v("w2qw"))]

    for fq in range(4):
        for qd in range(4):
            gqd = fq * 4 + qd
            slot = gqd % 2
            for ck in range(CK):
                w = p.F(("m1r", slot)) if ck == 0 else []
                if gqd == 0 and ck == 0:
                    w = w + W1QW + [("qst", 128), ("woc", 16), ("aPV", 1024)]
                if gqd == 1 and ck == 0:
                    w = w + [("qst", 128), ("aPV", 1024)] + W2QW + SEV_ALL
                p.emit("sync", (lambda e, s=slot, c=ck,
                                sr=w1qd[512 * gqd:512 * (gqd + 1),
                                        128 * ck:128 * (ck + 1)]:
                                e.dma_start_transpose(
                                    m1r[s][:, 512 * c:512 * (c + 1)], sr)),
                       wait=w, inc=((f"m1l{slot}", 16),))
            p.ready(("m1rd", slot), f"m1l{slot}")
            for fi in range(4):
                fg = gqd * 4 + fi
                fiq = qd * 4 + fi
                for jt in range(2):
                    bk = (2 * fiq + jt) % 4
                    for ck in range(CK):
                        w = []
                        if ck == 0:
                            w = p.R("xnTall", ("m1rd", slot)) + p.F(("bk", bk))
                        p.emit("tensor", (lambda e, b=bk, s=slot, c=ck, f=fi, j=jt:
                                          e.matmul(bank(b),
                                                   m1r[s][:, 512 * c + 128 * f:512 * c + 128 * (f + 1)],
                                                   xnT[:, 1024 * c + 512 * j:1024 * c + 512 * (j + 1)],
                                                   start=(c == 0), stop=(c == CK - 1))),
                               wait=w, inc=((("m1m", 1),) if ck == CK - 1 else ()))
                    p.ready(("bk", bk), "m1m")
                    p.emit("scalar", (lambda e, b=bk, fi_=fiq, j=jt, f=fg:
                                      e.activation(
                        out=hbuf[:, 1024 * fi_ + 512 * j:1024 * fi_ + 512 * (j + 1)],
                        in_=bank(b), func=AF.Gelu, bias=b1s[:, f:f + 1])),
                        wait=p.R(("bk", bk)) + CSTW, inc=(("gel", 1),))
                    p.freed(("bk", bk), "gel")
            p.rel[("m1r", slot)] = [("m1m", p.v("m1m"))]
        GEL = [("gel", p.v("gel"))]
        for cq in range(4):
            gcq = fq * 4 + cq
            for oct_ in range(2):
                goct = gcq * 2 + oct_
                slot = goct % 3
                for fc in range(8):
                    fcq = oct_ * 8 + fc
                    fgl = fq * 16 + fcq
                    w = p.F(("m2r", slot)) if fc == 0 else []
                    if goct == 0 and fc == 0:
                        w = w + W2QW + [("smm", 32), ("aAH", 32)]
                    if goct == 1 and fc == 0:
                        w = w + [("aAH", 32)]
                    p.emit("sync", (lambda e, s=slot, f=fc,
                                    sr=w2qd[512 * cq:512 * (cq + 1),
                                            128 * fgl:128 * (fgl + 1)]:
                                    e.dma_start_transpose(
                                        m2r[s][:, 512 * f:512 * (f + 1)], sr)),
                           wait=w, inc=((f"m2l{slot}", 16),))
                p.ready(("m2rd", slot), f"m2l{slot}")
                for pr2 in range(8):
                    ckq, jt = pr2 // 2, pr2 % 2
                    bk = pr2
                    for fc in range(8):
                        fcq = oct_ * 8 + fc
                        w = []
                        if fc == 0:
                            w += p.R(("m2rd", slot)) + GEL
                            if oct_ == 0:
                                w += p.F(("bk", bk))
                        p.emit("tensor", (lambda e, b=bk, s=slot, f=fc, fi_=fcq,
                                          j=jt, c=ckq, oc=oct_:
                                          e.matmul(bank(b),
                                                   m2r[s][:, 512 * f + 128 * c:512 * f + 128 * (c + 1)],
                                                   hbuf[:, 1024 * fi_ + 512 * j:1024 * fi_ + 512 * (j + 1)],
                                                   start=(oc == 0 and f == 0),
                                                   stop=(oc == 1 and f == 7))),
                               wait=w,
                               inc=((("m2m", 1),) if (oct_ == 1 and fc == 7) else ()))
                    if oct_ == 1:
                        p.ready(("bk", bk), "m2m")
                        ckg = cq * 4 + ckq
                        acc = accl if ckg < 8 else acch
                        ao = (ckg % 8) * 1024 + 512 * jt
                        if fq == 0:
                            p.emit("vector", (lambda e, b=bk, a=acc, o=ao:
                                              e.tensor_copy(out=a[:, o:o + 512],
                                                            in_=bank(b))),
                                   wait=p.R(("bk", bk)), inc=(("m2e", 1),))
                        else:
                            p.emit("vector", (lambda e, b=bk, a=acc, o=ao:
                                              e.tensor_add(out=a[:, o:o + 512],
                                                           in0=a[:, o:o + 512],
                                                           in1=bank(b))),
                                   wait=p.R(("bk", bk)), inc=(("m2e", 1),))
                        p.freed(("bk", bk), "m2e")
                p.rel[("m2r", slot)] = [("m2m", p.v("m2m"))] if oct_ == 1 else \
                    [("m2m", p.v("m2m") + (8 if oct_ == 0 else 0))]
        for key in [("hbuf", 0)]:
            p.rel[key] = [("m2m", p.v("m2m"))]
    M2E = [("m2e", p.v("m2e"))]

    for ck in range(CK):
        acc = accl if ck < 8 else acch
        ao = (ck % 8) * 1024
        slot = ck % 2
        p.emit("scalar", (lambda e, s=slot, sr=ssc[128 * ck:128 * (ck + 1), :]:
                          e.dma_start(out=stre[s][:, :], in_=sr)),
               wait=SEV_ALL + M2E + p.F(("stre", slot)), inc=((f"stl{slot}", 16),))
        p.ready(("strd", slot), f"stl{slot}")
        p.emit("scalar", (lambda e, a=acc, o=ao, c=ck: e.activation(
            out=gtmp[:, :], in_=a[:, o:o + 1024], func=AF.Silu,
            bias=b2s[:, c:c + 1])),
            wait=M2E + p.F("gtmp"), inc=(("sil", 1),))
        p.emit("vector", (lambda e, a=acc, o=ao, s=slot: e.tensor_tensor(
            out=a[:, o:o + 1024], in0=gtmp[:, :], in1=stre[s][:, :],
            op=ALU.mult)),
            wait=[("sil", p.v("sil"))] + p.R(("strd", slot)),
            inc=(("sgv", 1),))
        p.freed("gtmp", "sgv")
        p.freed(("stre", slot), "sgv")
    SGV = [("sgv", p.v("sgv"))]
    for tc in range(8):
        slot = tc % 2
        pst = psA if slot == 0 else psB
        pkeys = [("bk", 0 + 4 * slot), ("bk", 1 + 4 * slot),
                 ("bk", 2 + 4 * slot), ("bk", 3 + 4 * slot)]
        p.emit("sync", (lambda e, s=slot, sr=xb[128 * tc:128 * (tc + 1), :]:
                        e.dma_start(out=xoring[:, 2048 * s:2048 * (s + 1)], in_=sr)),
               wait=SGV + p.F(("xo", slot)), inc=((f"xo{slot}", 16),))
        p.ready(("xod", slot), f"xo{slot}")
        for ck in range(CK):
            acc = accl if ck < 8 else acch
            ao = (ck % 8) * 1024
            w = SGV + CSTW
            if ck == 0:
                w += p.F(*pkeys)
            p.emit("tensor", (lambda e, pp=pst, a=acc, o=ao, t=tc, c=ck:
                              e.transpose(pp[:, 128 * c:128 * (c + 1)],
                                          a[:, o + 128 * t:o + 128 * (t + 1)],
                                          identf[:, :])),
                   wait=w if ck == 0 else (),
                   inc=((("otp", 1),) if ck == CK - 1 else ()))
        p.emit("vector", (lambda e, pp=pst, s=slot: e.tensor_add(
            out=xoring[:, 2048 * s:2048 * (s + 1)],
            in0=xoring[:, 2048 * s:2048 * (s + 1)], in1=pp[:, :])),
            wait=[("otp", p.v("otp"))] + p.R(("xod", slot)),
            inc=(("oadd", 1),))
        for k in pkeys:
            p.freed(k, "oadd")
        p.emit("sync", (lambda e, s=slot, d=out_d[128 * tc:128 * (tc + 1), :]:
                        e.dma_start(out=d, in_=xoring[:, 2048 * s:2048 * (s + 1)])),
               wait=[("oadd", p.v("oadd"))], inc=(("outw", 16),))
        p.freed(("xo", slot), "outw")
    p.emit("sync", None, wait=[("outw", 8 * 16)])

    sems = {}
    for name in p.sems_used():
        sems[name] = ctx.enter_context(nc.semaphore(name))

    def replay(engine_obj, key):
        for fn, waits, incs in p.ops[key]:
            for s, v in waits:
                engine_obj.wait_ge(sems[s], v)
            if fn is not None:
                ins = fn(engine_obj)
                for s, n in incs:
                    r = ins.then_inc(sems[s], n)
                    ins = r if r is not None else ins

    with nc.Block() as block:
        @block.sync
        def _(sync):
            replay(sync, "sync")

        @block.scalar
        def _(scalar):
            replay(scalar, "scalar")

        @block.vector
        def _(vector):
            replay(vector, "vector")

        @block.tensor
        def _(tensor):
            replay(tensor, "tensor")

        @block.gpsimd
        def _(gpsimd):
            replay(gpsimd, "gpsimd")

    ctx.close()
    return nc


_NC_CACHE = {}


def kernel(L0_attn_x_q, L0_attn_Wq, L0_attn_Wk, L0_attn_Wv, L0_attn_Wo,
           L0_ffn_w1, L0_ffn_b1, L0_ffn_w2, L0_ffn_b2):
    x = np.ascontiguousarray(np.asarray(L0_attn_x_q, dtype=np.float32))
    wqa = np.ascontiguousarray(np.asarray(L0_attn_Wq, dtype=np.float32).reshape(C, HQ * D))
    wka = np.ascontiguousarray(np.asarray(L0_attn_Wk, dtype=np.float32).reshape(C, HKV * D))
    wva = np.ascontiguousarray(np.asarray(L0_attn_Wv, dtype=np.float32).reshape(C, HKV * D))
    woa = np.ascontiguousarray(np.asarray(L0_attn_Wo, dtype=np.float32))
    w1a = np.ascontiguousarray(np.asarray(L0_ffn_w1, dtype=np.float32))
    w2a = np.ascontiguousarray(np.asarray(L0_ffn_w2, dtype=np.float32))
    b1cv = np.ascontiguousarray(np.asarray(L0_ffn_b1, dtype=np.float32).reshape(FF // 128, 128).T)
    b2cv = np.ascontiguousarray(np.asarray(L0_ffn_b2, dtype=np.float32).reshape(C // 128, 128).T)
    identf = np.eye(128, dtype=np.float32)
    onescb = np.ones((128, 1), dtype=ml_dtypes.bfloat16)
    onesrb = np.ones((1, 128), dtype=ml_dtypes.bfloat16)
    onescf = np.ones((128, 1), dtype=np.float32)
    onesrf = np.ones((1, 128), dtype=np.float32)
    epsc = np.zeros((128, 4), dtype=np.float32)
    epsc[:, 0] = np.float32(EPS_LN)
    epsc[:, 1] = np.float32(EPS_QK)
    epsc[:, 2] = np.float32(EPS_QK_128)

    if "nc" not in _NC_CACHE:
        _NC_CACHE["nc"] = build_nc()
    nc = _NC_CACHE["nc"]

    in_maps = []
    for core in range(8):
        b, qt = core // 4, core % 4
        xrot = np.ascontiguousarray(np.roll(x[b], -qt * OWN, axis=0))
        in_maps.append({
            "xb": xrot, "wq": wqa, "wk": wka, "wv": wva, "wo": woa,
            "w1": w1a, "w2": w2a, "b1c": b1cv, "b2c": b2cv,
            "identf": identf, "onescb": onescb, "onesrb": onesrb,
            "onescf": onescf, "onesrf": onesrf, "epsc": epsc,
        })
    res = run_bass_kernel_spmd(nc, in_maps, core_ids=list(range(8)))
    out = np.empty((B, T, C), dtype=np.float32)
    for core in range(8):
        b, qt = core // 4, core % 4
        out[b, qt * OWN:(qt + 1) * OWN, :] = res.results[core]["out"]
    return out


# version 25
# speedup vs baseline: 1.0046x; 1.0046x over previous
import numpy as np
import ml_dtypes
from contextlib import ExitStack
import concourse.bass as bass
import concourse.mybir as mybir
from concourse.bass_utils import run_bass_kernel_spmd

F32 = mybir.dt.float32
BF16 = mybir.dt.bfloat16
U8 = mybir.dt.uint8
AF = mybir.ActivationFunctionType
ALU = mybir.AluOpType
AX = mybir.AxisListType

B, T, C, HQ, HKV, D, FF = 2, 4096, 2048, 16, 4, 128, 8192
CK = C // 128
OWN = 1024
NQ = 4
EPS_LN = 1e-5
EPS_QK = float(np.float32(1e-6))
EPS_QK_128 = float(np.float32(1e-6) * 128.0)

DEBUG_TAPS = False


class Prog:
    ENGINES = ("sync", "scalar", "vector", "tensor", "gpsimd")

    def __init__(self):
        self.ops = {e: [] for e in self.ENGINES}
        self.cnt = {}
        self.rdy = {}
        self.rel = {}

    def v(self, sem):
        return self.cnt.get(sem, 0)

    def emit(self, eng, fn, wait=(), inc=()):
        wl = []
        seen = {}
        for s, n in wait:
            if n > 0 and seen.get(s, -1) < n:
                seen[s] = n
        for s, n in seen.items():
            wl.append((s, n))
        for s, n in inc:
            self.cnt[s] = self.cnt.get(s, 0) + n
        self.ops[eng].append((fn, tuple(wl), tuple(inc)))

    def mark(self, table, key, sem):
        table[key] = [(sem, self.v(sem))]

    def ready(self, key, sem):
        self.mark(self.rdy, key, sem)

    def freed(self, key, sem):
        self.mark(self.rel, key, sem)

    def R(self, *keys):
        out = []
        for k in keys:
            out += self.rdy.get(k, [])
        return out

    def F(self, *keys):
        out = []
        for k in keys:
            out += self.rel.get(k, [])
        return out

    def sems_used(self):
        names = set(self.cnt)
        for e in self.ENGINES:
            for _, waits, _ in self.ops[e]:
                for s, _ in waits:
                    names.add(s)
        return sorted(names)


def build_nc():
    nc = bass.Bass("TRN2", target_bir_lowering=False,
                   detect_race_conditions=False)

    kind_scr = "ExternalOutput" if DEBUG_TAPS else "Internal"
    xb = nc.dram_tensor("xb", [T, C], F32, kind="ExternalInput").ap()
    wq = nc.dram_tensor("wq", [C, HQ * D], F32, kind="ExternalInput").ap()
    wk = nc.dram_tensor("wk", [C, HKV * D], F32, kind="ExternalInput").ap()
    wv = nc.dram_tensor("wv", [C, HKV * D], F32, kind="ExternalInput").ap()
    wo = nc.dram_tensor("wo", [C, C], F32, kind="ExternalInput").ap()
    w1 = nc.dram_tensor("w1", [FF, C], F32, kind="ExternalInput").ap()
    w2 = nc.dram_tensor("w2", [C, FF], F32, kind="ExternalInput").ap()
    b1c = nc.dram_tensor("b1c", [128, FF // 128], F32, kind="ExternalInput").ap()
    b2c = nc.dram_tensor("b2c", [128, C // 128], F32, kind="ExternalInput").ap()
    identf_d = nc.dram_tensor("identf", [128, 128], F32, kind="ExternalInput").ap()
    onescb_d = nc.dram_tensor("onescb", [128, 1], BF16, kind="ExternalInput").ap()
    onesrb_d = nc.dram_tensor("onesrb", [1, 128], BF16, kind="ExternalInput").ap()
    onescf_d = nc.dram_tensor("onescf", [128, 1], F32, kind="ExternalInput").ap()
    onesrf_d = nc.dram_tensor("onesrf", [1, 128], F32, kind="ExternalInput").ap()
    epsc_d = nc.dram_tensor("epsc", [128, 4], F32, kind="ExternalInput").ap()
    out_d = nc.dram_tensor("out", [OWN, C], F32, kind="ExternalOutput").ap()

    xnsc = nc.dram_tensor("xnsc", [T, C], BF16, kind=kind_scr).ap()
    w1qd = nc.dram_tensor("w1qd", [FF, C], BF16, kind=kind_scr).ap()
    w2qd = nc.dram_tensor("w2qd", [C, FF], BF16, kind=kind_scr).ap()
    ssc = nc.dram_tensor("ssc", [C, OWN], BF16, kind=kind_scr).ap()

    ctx = ExitStack()
    ARENA_BYTES = 204800
    arena = nc.alloc_sbuf_tensor("arena", [128, ARENA_BYTES], U8)
    base = nc.lookup_mloc(arena).addr

    def sub(name, shape, dtype, off):
        nbytes = int(np.prod(shape[1:])) * mybir.dt.size(dtype)
        assert off % 32 == 0, (name, off)
        assert off + nbytes <= ARENA_BYTES, (name, off, nbytes)
        return nc.alloc_sbuf_tensor_at(name, shape, dtype, offset=base + off)

    A0, B0, C0, DE, FB = 0, 65536, 98304, 131072, 188416
    kbuf = sub("kbuf", [128, 16384], BF16, A0)
    vbuf = sub("vbuf", [128, 16384], BF16, A0 + 32768)
    wqb = sub("wqb", [128, 32768], BF16, A0)
    wob = sub("wob", [128, 32768], BF16, A0)
    hbuf = sub("hbuf", [128, 16384], BF16, A0)
    acch = sub("acch", [128, 8192], F32, A0 + 32768)
    xoring = sub("xoring", [128, 4096], F32, A0)
    qbuf = sub("qbuf", [128, 16384], BF16, B0)
    accl = sub("accl", [128, 8192], F32, B0)
    xnT = sub("xnT", [128, 16384], BF16, C0)
    abuf = sub("abuf", [128, 16384], BF16, C0)
    xr = [sub(f"xr{i}", [128, 2048], F32, DE + 8192 * i) for i in range(2)]
    xnr = [sub(f"xnr{i}", [128, 2048], BF16, DE + 16384 + 4096 * i) for i in range(2)]
    wqr = [sub(f"wqr{i}", [128, 2048], F32, DE + 24576 + 8192 * i) for i in range(2)]
    wkvb = sub("wkvb", [128, 8192], BF16, DE + 24576)
    wkvr = [sub(f"wkvr{i}", [128, 512], F32, DE + 40960 + 2048 * i) for i in range(2)]
    ws1r = [sub(f"ws1r{i}", [128, 512], F32, DE + 45056 + 2048 * i) for i in range(2)]
    ws2r = [sub(f"ws2r{i}", [128, 512], F32, DE + 49152 + 2048 * i) for i in range(2)]
    qwf = [sub(f"qwf{i}", [128, 2048], F32, DE + 8192 * i) for i in range(2)]
    qc2 = sub("qc2", [128, 2048], BF16, DE + 16384)
    qqt = [sub(f"qqt{i}", [128, 2048], BF16, DE + 20480 + 4096 * i) for i in range(2)]
    ptile = sub("ptile", [128, 4096], BF16, DE + 28672)
    rzbb = [sub(f"rzbb{i}", [128, 512], BF16, DE + 36864 + 1024 * i) for i in range(2)]
    wor = [sub(f"wor{i}", [128, 2048], F32, DE + 8192 * i) for i in range(2)]
    stv = [sub(f"stv{i}", [128, 512], BF16, DE + 16384 + 1024 * i) for i in range(2)]
    m1r = [sub(f"m1r{i}", [128, 8192], BF16, DE + 16384 * i) for i in range(2)]
    m2r = [sub(f"m2r{i}", [128, 4096], BF16, DE + 32768 + 8192 * i) for i in range(3)]
    stre = [sub(f"stre{i}", [128, 1024], BF16, DE + 49152 + 2048 * i) for i in range(2)]
    gtmp = sub("gtmp", [128, 1024], F32, DE + 53248)
    identf = sub("identf_s", [128, 128], F32, FB + 0)
    onescb = sub("onescb_s", [128, 1], BF16, FB + 512)
    onesrb = sub("onesrb_s", [1, 128], BF16, FB + 544)
    onescf = sub("onescf_s", [128, 1], F32, FB + 800)
    onesrf = sub("onesrf_s", [1, 128], F32, FB + 832)
    b1s = sub("b1s", [128, 64], F32, FB + 1344)
    b2s = sub("b2s", [128, 16], F32, FB + 1600)
    lnS = {k: sub(f"ln_{k}", [128, 8], F32, FB + 1664 + 32 * i)
           for i, k in enumerate(["sm", "sq", "m", "e", "v", "r", "nm", "t"])}
    sdump = sub("sdump", [128, 1024], BF16, FB + 1952)
    praw = [sub(f"praw{i}", [128, 512], BF16, FB + 4000 + 1024 * i) for i in range(2)]
    praw.append(sub("praw2", [128, 512], BF16, FB + 7072))
    ksq = sub("ksq", [128, 512], BF16, FB + 6048)
    rbs = [sub("rb0", [1, 512], BF16, FB + 8096),
           sub("rb1", [1, 512], BF16, FB + 9120)]
    rb = rbs[0]
    sd = sub("sd", [1, 512], F32, FB + 13440)
    w1p = sub("w1p", [128, 256], F32, FB + 10144)
    w2p = sub("w2p", [128, 256], F32, FB + 11168)
    hs1b = sub("hs1b", [128, 1], F32, FB + 12192)
    hs1n = sub("hs1n", [128, 1], F32, FB + 12224)
    hs2b = sub("hs2b", [128, 1], F32, FB + 12256)
    hs2n = sub("hs2n", [128, 1], F32, FB + 12288)
    wtot = sub("wtot", [128, 1], F32, FB + 12320)
    hst = sub("hst", [1, 1], F32, FB + 12352)
    epss = sub("epss", [128, 4], F32, FB + 12384)
    spacebuf = sub("spacebuf", [128, 512], F32, DE + 53248)
    azb = sub("azb", [128, 512], BF16, FB + 12416)

    psA = ctx.enter_context(nc.psum_tensor([128, 2048], F32))
    psB = ctx.enter_context(nc.psum_tensor([128, 2048], F32))

    def bank(i):
        t = psA if i < 4 else psB
        j = i % 4
        return t[:, j * 512:(j + 1) * 512]

    p = Prog()

    p.emit("vector", (lambda e: e.memset(spacebuf[:, :], 0.0)))

    def vsp():
        p.emit("vector", (lambda e: e.tensor_copy(out=spacebuf[:, :],
                                                  in_=spacebuf[:, :])))

    for nm, dst, src in [("identf", identf, identf_d), ("onescb", onescb, onescb_d),
                         ("onesrb", onesrb, onesrb_d), ("onescf", onescf, onescf_d),
                         ("onesrf", onesrf, onesrf_d), ("b1s", b1s, b1c),
                         ("b2s", b2s, b2c), ("epss", epss, epsc_d)]:
        p.emit("sync", (lambda e, d=dst, s=src: e.dma_start(out=d[:, :], in_=s)),
               inc=(("cst", 16),))
    p.ready("cst", "cst")
    CSTW = p.R("cst")

    w1sl = [(w1[128 * r:128 * (r + 1), 512 * cb:512 * (cb + 1)], 4 * r + cb)
            for r in range(64) for cb in range(4)]
    w2sl = [(w2[128 * r:128 * (r + 1), 512 * fb:512 * (fb + 1)], 16 * r + fb)
            for r in range(16) for fb in range(16)]
    sc = {"i1": 0, "i2": 0}

    def emit_scale(n):
        for _ in range(n):
            i = sc["i1"]
            if i < 256:
                sc["i1"] += 1
                src, col = w1sl[i]
                slot = i % 2
                p.emit("sync",
                       (lambda e, d=ws1r[slot], s=src: e.dma_start(out=d[:, :], in_=s)),
                       wait=CSTW + p.F(("ws1", slot)), inc=((f"s1l{slot}", 16),))
                p.ready(("ws1d", slot), f"s1l{slot}")
                p.emit("scalar", (lambda e, sl=slot, cl=col: e.activation(
                    out=sdump[:, 0:512], in_=ws1r[sl][:, :], func=AF.Abs,
                    accum_out=w1p[:, cl:cl + 1])),
                    wait=p.R(("ws1d", slot)), inc=(("w1abs", 1),))
                p.freed(("ws1", slot), "w1abs")
            j = sc["i2"]
            if j < 256:
                sc["i2"] += 1
                src, col = w2sl[j]
                slot = j % 2
                p.emit("sync",
                       (lambda e, d=ws2r[slot], s=src: e.dma_start(out=d[:, :], in_=s)),
                       wait=CSTW + p.F(("ws2", slot)), inc=((f"s2l{slot}", 16),))
                p.ready(("ws2d", slot), f"s2l{slot}")
                p.emit("vector", (lambda e, sl=slot, cl=col: e.tensor_reduce(
                    out=w2p[:, cl:cl + 1], in_=ws2r[sl][:, :], axis=AX.X,
                    op=ALU.add, apply_absolute_value=True)),
                    wait=p.R(("ws2d", slot)), inc=(("w2abs", 1),))
                p.freed(("ws2", slot), "w2abs")

    proj_pend = {"B": [], "D": []}

    def proj_flush(nB=99, nD=99):
        for _ in range(min(nD, len(proj_pend["D"]))):
            proj_pend["D"].pop(0)()
        for _ in range(min(nB, len(proj_pend["B"]))):
            proj_pend["B"].pop(0)()
        if nD >= 99:
            while proj_pend["D"]:
                proj_pend["D"].pop(0)()

    def emit_proj_norm(gp, bk, dst, dst_off, is_q):
        slot = gp % 3
        rslot = gp % 2
        p.emit("scalar", (lambda e, b=bk: e.activation(
            out=ksq[:, :], in_=bank(b), func=AF.Square)),
            wait=p.R(("bk", bk)) + p.F("ksq"), inc=(("psq", 1),))
        psq_now = [("psq", p.v("psq"))]
        p.emit("scalar", (lambda e, b=bk, sl=slot: e.copy(
            out=praw[sl][:, :], in_=bank(b))),
            wait=p.F(("praw", slot)), inc=(("prw", 1),))
        p.freed(("bk", bk), "prw")
        prw_now = [("prw", p.v("prw"))]

        def stageBC():
            ssb = 2 + gp % 2
            p.emit("tensor", (lambda e, b=ssb: e.matmul(
                bank(b)[0:1, :], onescb[:, 0:1], ksq[:, :],
                start=True, stop=True)),
                wait=psq_now + p.F(("bk", ssb)), inc=(("pss", 1),))
            p.freed("ksq", "pss")
            pss_now = [("pss", p.v("pss"))]
            if is_q:
                p.emit("scalar", (lambda e, b=ssb: e.activation(
                    out=sd[0:1, :], in_=bank(b)[0:1, :], func=AF.Sqrt,
                    bias=epss[0:1, 2:3])),
                    wait=pss_now + p.F("sd"), inc=(("psd", 1),))
            else:
                p.emit("scalar", (lambda e, b=ssb: e.activation(
                    out=sd[0:1, :], in_=bank(b)[0:1, :], func=AF.Sqrt,
                    bias=epss[0:1, 1:2], scale=1.0 / 128)),
                    wait=pss_now + p.F("sd"), inc=(("psd", 1),))
            p.freed(("bk", ssb), "psd")
            psd_now = [("psd", p.v("psd"))]

            def _recip_rb(e, sl=rslot):
                with nc.allow_low_precision(reason="1/x rounded once to bf16"):
                    return e.reciprocal(out=rbs[sl][0:1, :], in_=sd[0:1, :])
            p.emit("vector", _recip_rb,
                   wait=psd_now + p.F(("rb", rslot)), inc=(("prb", 1),))
            p.freed("sd", "prb")
            prb_now = [("prb", p.v("prb"))]

            def stageDE():
                bcb = 4 + gp % 2
                p.emit("tensor", (lambda e, b=bcb, sl=rslot: e.matmul(
                    bank(b), onesrb[0:1, :], rbs[sl][0:1, :],
                    start=True, stop=True)),
                    wait=prb_now + p.F(("bk", bcb)), inc=(("pbc", 1),))
                p.freed(("rb", rslot), "pbc")
                p.emit("vector", (lambda e, b=bcb, sl=slot, d=dst, o=dst_off:
                                  e.tensor_tensor(out=d[:, o:o + 512],
                                                  in0=praw[sl][:, :],
                                                  in1=bank(b), op=ALU.mult)),
                       wait=[("pbc", p.v("pbc"))] + prw_now,
                       inc=(("ptt", 1),))
                p.freed(("praw", slot), "ptt")
                p.freed(("bk", bcb), "ptt")
            proj_pend["D"].append(stageDE)

        proj_pend["B"].append(stageBC)

    def ln_steps(q):
        qo = q * OWN
        steps = []

        def tileA(lt):
            def go():
                tt = q * 8 + lt
                slot = tt % 2
                p.emit("scalar", (lambda e, d=xr[slot],
                                  sr=xb[qo + 128 * lt:qo + 128 * (lt + 1), :]:
                                  e.dma_start(out=d[:, :], in_=sr)),
                       wait=p.F(("xr", slot)), inc=((f"xl{slot}", 16),))
                p.ready(("xrd", slot), f"xl{slot}")
                xw = p.R(("xrd", slot))
                p.emit("scalar", (lambda e, sl=slot, c=lt: e.activation(
                    out=sdump[:, :], in_=xr[sl][:, 0:1024], func=AF.Square,
                    accum_out=lnS["t"][:, c:c + 1])), wait=xw + p.F("lnrowA"))
                p.emit("scalar", (lambda e, sl=slot, c=lt: e.activation(
                    out=sdump[:, :], in_=xr[sl][:, 1024:2048], func=AF.Square,
                    accum_out=lnS["sq"][:, c:c + 1])), inc=(("lnsq", 1),))
                p.emit("vector", (lambda e, sl=slot, c=lt: e.tensor_reduce(
                    out=lnS["sm"][:, c:c + 1], in_=xr[sl][:, :], axis=AX.X,
                    op=ALU.add)), wait=xw + p.F("lnrowA"), inc=(("lnsm", 1),))
                p.rel[("xr", slot)] = [("lnsq", p.v("lnsq")), ("lnsm", p.v("lnsm"))]
            return go

        for lt in range(8):
            steps.append(tileA(lt))

        def stats():
            w0 = [("lnsq", p.v("lnsq")), ("lnsm", p.v("lnsm"))] + p.F("lnrowB")
            vsp()
            p.emit("vector", (lambda e: e.tensor_add(
                out=lnS["sq"][:, :], in0=lnS["sq"][:, :], in1=lnS["t"][:, :])),
                wait=w0)
            p.emit("vector", (lambda e: e.tensor_scalar_mul(
                out=lnS["m"][:, :], in0=lnS["sm"][:, :], scalar1=1.0 / 2048)))
            vsp()
            p.emit("vector", (lambda e: e.tensor_scalar_mul(
                out=lnS["e"][:, :], in0=lnS["sq"][:, :], scalar1=1.0 / 2048)))
            vsp()
            p.emit("vector", (lambda e: e.tensor_mul(
                out=lnS["t"][:, :], in0=lnS["m"][:, :], in1=lnS["m"][:, :])))
            vsp()
            p.emit("vector", (lambda e: e.tensor_sub(
                out=lnS["v"][:, :], in0=lnS["e"][:, :], in1=lnS["t"][:, :])),
                inc=(("lnv", 1),))
            p.emit("scalar", (lambda e: e.activation(
                out=lnS["v"][:, :], in_=lnS["v"][:, :], func=AF.Sqrt,
                bias=epss[:, 0:1])), wait=[("lnv", p.v("lnv"))],
                inc=(("lnsd", 1),))
            p.emit("vector", (lambda e: e.reciprocal(
                out=lnS["r"][:, :], in_=lnS["v"][:, :])),
                wait=[("lnsd", p.v("lnsd"))])
            vsp()
            p.emit("vector", (lambda e: e.tensor_mul(
                out=lnS["nm"][:, :], in0=lnS["m"][:, :], in1=lnS["r"][:, :])))
            vsp()
            p.emit("vector", (lambda e: e.tensor_scalar_mul(
                out=lnS["nm"][:, :], in0=lnS["nm"][:, :], scalar1=-1.0)),
                inc=(("lnst", 1),))
            p.freed("lnrowA", "lnst")

        steps.append(stats)

        def tileB(lt):
            def go():
                LNST = [("lnst", p.v("lnst"))]
                slot = lt % 2
                p.emit("scalar", (lambda e, d=xr[slot],
                                  sr=xb[qo + 128 * lt:qo + 128 * (lt + 1), :]:
                                  e.dma_start(out=d[:, :], in_=sr)),
                       wait=p.F(("xr", slot)), inc=((f"xl{slot}", 16),))
                p.ready(("xrd", slot), f"xl{slot}")
                p.emit("scalar", (lambda e, sl=slot, c=lt: e.activation(
                    out=xnr[sl][:, :], in_=xr[sl][:, :], func=AF.Identity,
                    scale=lnS["r"][:, c:c + 1], bias=lnS["nm"][:, c:c + 1])),
                    wait=LNST + p.R(("xrd", slot)) + p.F(("xnr", slot)),
                    inc=(("lnap", 1),))
                p.freed(("xr", slot), "lnap")
                p.emit("scalar", (lambda e, sl=slot,
                                  d=xnsc[qo + 128 * lt:qo + 128 * (lt + 1), :]:
                                  e.dma_start(out=d, in_=xnr[sl][:, :])),
                       wait=[("lnap", p.v("lnap"))], inc=(("xnw", 16),))
                p.freed(("xnr", slot), "xnw")
                if lt == 7:
                    p.freed("lnrowB", "lnap")
                    p.ready(("lnq", q), "xnw")
            return go

        for lt in range(8):
            steps.append(tileB(lt))
        return steps

    gp_counter = {"gp": 0}

    filler = {"steps": []}

    sc_budget = {"n": 130}

    def fill(n=1):
        for _ in range(n):
            if filler["steps"]:
                filler["steps"].pop(0)()
            elif sc_budget["n"] > 0:
                sc_budget["n"] -= 1
                emit_scale(1)

    def drain_filler():
        while filler["steps"]:
            filler["steps"].pop(0)()

    def p1_quarter(q):
        qo = q * OWN
        if q == 0:
            for ck in range(CK):
                slot = ck % 2
                p.emit("scalar", (lambda e, d=wqr[slot],
                                  sr=wq[128 * ck:128 * (ck + 1), :]:
                                  e.dma_start(out=d[:, :], in_=sr)),
                       wait=CSTW + p.F(("wqr", slot)), inc=((f"wql{slot}", 16),))
                p.ready(("wqrd", slot), f"wql{slot}")
                p.emit("vector", (lambda e, sl=slot, c=ck: e.tensor_copy(
                    out=wqb[:, 2048 * c:2048 * (c + 1)], in_=wqr[sl][:, :])),
                    wait=p.R(("wqrd", slot)), inc=(("wqc", 1),))
                p.freed(("wqr", slot), "wqc")
            p.ready("wqb", "wqc")
            for st in ln_steps(0):
                st()
        else:
            drain_filler()
        if q + 1 < NQ:
            filler["steps"] = ln_steps(q + 1)

        for ck in range(CK):
            p.emit("sync", (lambda e, c=ck,
                            sr=xnsc[qo:qo + OWN, 128 * ck:128 * (ck + 1)]:
                            e.dma_start_transpose(
                                xnT[:, 1024 * c:1024 * (c + 1)], sr)),
                   wait=p.R(("lnq", q)) + p.F(("xnT", ck)),
                   inc=(("xta", 16),))
        p.ready("xnTall", "xta")

        if q == 0:
            for pr in range(HQ * 2):
                h, jt = pr // 2, pr % 2
                bk = pr % 2
                for ck in range(CK):
                    w = []
                    if ck == 0:
                        w += p.R("xnTall", "wqb") + p.F(("bk", bk))
                    p.emit("tensor", (lambda e, b=bk, c=ck, hh=h, j=jt:
                                      e.matmul(bank(b),
                                               wqb[:, 2048 * c + 128 * hh:2048 * c + 128 * (hh + 1)],
                                               xnT[:, 1024 * c + 512 * j:1024 * c + 512 * (j + 1)],
                                               start=(c == 0), stop=(c == CK - 1))),
                           wait=w, inc=((("pmm", 1),) if ck == CK - 1 else ()))
                p.ready(("bk", bk), "pmm")
                proj_flush(nB=1, nD=1 if len(proj_pend["D"]) >= 2 else 0)
                emit_proj_norm(gp_counter["gp"], bk, qbuf, h * 1024 + jt * 512, True)
                gp_counter["gp"] += 1
            p.ready("qdone", "ptt")

        for ck in range(CK):
            slot = ck % 2
            p.emit("scalar", (lambda e, d=wkvr[slot],
                              sr=wk[128 * ck:128 * (ck + 1), :]:
                              e.dma_start(out=d[:, :], in_=sr)),
                   wait=CSTW + p.F(("wkvr", slot)), inc=((f"wvl{slot}", 16),))
            p.ready(("wkvrd", slot), f"wvl{slot}")
            w = p.R(("wkvrd", slot)) + p.F(("wkvb", ck))
            if q == 0:
                w += [("wqc", 16)]
            p.emit("vector", (lambda e, sl=slot, c=ck: e.tensor_copy(
                out=wkvb[:, 512 * c:512 * (c + 1)], in_=wkvr[sl][:, :])),
                wait=w, inc=(("wkc", 1),))
            p.freed(("wkvr", slot), "wkc")
        for pr in range(HKV * 2):
            kv, jt = pr // 2, pr % 2
            bk = pr % 2
            WKB = [("wkc", p.v("wkc"))]
            for ck in range(CK):
                w = []
                if ck == 0:
                    w = p.R("xnTall") + WKB + p.F(("bk", bk))
                p.emit("tensor", (lambda e, b=bk, c=ck, k=kv, j=jt:
                                  e.matmul(bank(b),
                                           wkvb[:, 512 * c + 128 * k:512 * c + 128 * (k + 1)],
                                           xnT[:, 1024 * c + 512 * j:1024 * c + 512 * (j + 1)],
                                           start=(c == 0), stop=(c == CK - 1))),
                       wait=w, inc=((("pmm", 1),) if ck == CK - 1 else ()))
                if ck % 2 == 1:
                    fill(1)
            p.ready(("bk", bk), "pmm")
            proj_flush(nB=1, nD=1 if len(proj_pend["D"]) >= 2 else 0)
            emit_proj_norm(gp_counter["gp"], bk,
                           kbuf, kv * 4096 + q * 1024 + jt * 512, False)
            gp_counter["gp"] += 1
        KMM_END = p.v("pmm")

        proj_flush(nB=9, nD=len(proj_pend["D"]))
        for ck in range(CK):
            slot = ck % 2
            p.emit("scalar", (lambda e, d=wkvr[slot],
                              sr=wv[128 * ck:128 * (ck + 1), :]:
                              e.dma_start(out=d[:, :], in_=sr)),
                   wait=p.F(("wkvr", slot)), inc=((f"wvl{slot}", 16),))
            p.ready(("wkvrd", slot), f"wvl{slot}")
            p.emit("vector", (lambda e, sl=slot, c=ck: e.tensor_copy(
                out=wkvb[:, 512 * c:512 * (c + 1)], in_=wkvr[sl][:, :])),
                wait=p.R(("wkvrd", slot)) + [("pmm", KMM_END)],
                inc=(("wkc", 1),))
            p.freed(("wkvr", slot), "wkc")
        for kc in range(8):
            bk = kc % 2
            WKB = [("wkc", p.v("wkc"))]
            for ck in range(CK):
                w = []
                if ck == 0:
                    w = p.R("xnTall") + WKB + p.F(("bk", bk))
                p.emit("tensor", (lambda e, b=bk, c=ck, k=kc:
                                  e.matmul(bank(b),
                                           xnT[:, 1024 * c + 128 * k:1024 * c + 128 * (k + 1)],
                                           wkvb[:, 512 * c:512 * (c + 1)],
                                           start=(c == 0), stop=(c == CK - 1))),
                       wait=w, inc=((("pmm", 1),) if ck == CK - 1 else ()))
                fill(1)
            p.ready(("bk", bk), "pmm")
            kcg = q * 8 + kc
            p.emit("scalar", (lambda e, b=bk, o=512 * kcg: e.copy(
                out=vbuf[:, o:o + 512], in_=bank(b))),
                wait=p.R(("bk", bk)), inc=(("vev", 1),))
            p.freed(("bk", bk), "vev")
        proj_flush()
        VMM_END = p.v("pmm")
        for ck in range(CK):
            p.rel[("xnT", ck)] = [("pmm", VMM_END)]
            p.rel[("wkvb", ck)] = [("pmm", VMM_END)]

    for q in range(NQ):
        p1_quarter(q)
    drain_filler()
    proj_flush()

    KV_READY = [("ptt", p.v("ptt")), ("vev", p.v("vev"))]

    NCH = 32

    SBANKS = (0, 1, 4, 5)

    def emit_pair(pr):
        h, jt = pr // 2, pr % 2
        kv = h // 4
        qoff = h * 1024 + jt * 512
        pvb = 2 + pr % 2
        zbk = 6 + pr % 2

        def emit_S(c):
            gc = pr * NCH + c
            sb = SBANKS[gc % 4]
            w = p.F(("bk", sb))
            if c == 0:
                w += KV_READY + p.F(("bk", pvb), ("bk", zbk))
                if pr >= 6:
                    w += [("hsx", 4)]
            p.emit("tensor", (lambda e, b=sb, k=kv, cc=c, qo=qoff:
                              e.matmul(bank(b),
                                       kbuf[:, k * 4096 + 128 * cc:k * 4096 + 128 * (cc + 1)],
                                       qbuf[:, qo:qo + 512],
                                       start=True, stop=True)),
                   wait=w, inc=(("aS", 1),))
            p.ready(("bk", sb), "aS")
            pslot = gc % 8
            p.emit("scalar", (lambda e, b=sb, sl=pslot: e.activation(
                out=ptile[:, 512 * sl:512 * (sl + 1)], in_=bank(b), func=AF.Exp)),
                wait=p.R(("bk", sb)) + p.F(("pt", pslot)), inc=(("aE", 1),))
            p.freed(("bk", sb), "aE")
            aE_now = [("aE", p.v("aE"))]
            if c == 0:
                p.emit("vector", (lambda e, sl=pslot: e.tensor_copy(
                    out=azb[:, :], in_=ptile[:, 512 * sl:512 * (sl + 1)])),
                    wait=aE_now + p.F("azb"), inc=(("aZv", 1),))
            else:
                p.emit("vector", (lambda e, sl=pslot: e.tensor_add(
                    out=azb[:, :], in0=azb[:, :],
                    in1=ptile[:, 512 * sl:512 * (sl + 1)])),
                    wait=aE_now, inc=(("aZv", 1),))

        def emit_PV(c):
            gc = pr * NCH + c
            pslot = gc % 8
            p.emit("tensor", (lambda e, b=pvb, sl=pslot, cc=c, k=kv: e.matmul(
                bank(b), vbuf[:, 512 * cc + 128 * k:512 * cc + 128 * (k + 1)],
                ptile[:, 512 * sl:512 * (sl + 1)],
                start=(cc == 0), stop=(cc == NCH - 1))),
                wait=[("aE", ae_mark[c])], inc=(("aPV", 1),))
            p.rel[("pt", pslot)] = [("aPV", p.v("aPV")), ("aZv", azv_mark[c])]

        ae_mark = {}
        azv_mark = {}
        for c in range(NCH):
            emit_S(c)
            ae_mark[c] = p.v("aE")
            azv_mark[c] = p.v("aZv")
            if c >= 4:
                emit_PV(c - 4)
            if c % 2 == 1:
                if pr < 6:
                    emit_scale(3)
                else:
                    emit_quant_ops(1)
        for c in range(NCH - 4, NCH):
            emit_PV(c)
        p.ready(("bk", pvb), "aPV")
        p.ready(("bk", pvb), "aPV")
        p.emit("tensor", (lambda e, b=zbk: e.matmul(
            bank(b)[0:1, :], onescb[:, 0:1], azb[:, :], start=True, stop=True)),
            wait=[("aZv", p.v("aZv"))] + p.F(("bk", zbk)), inc=(("aZ", 1),))
        p.ready(("bk", zbk), "aZ")
        p.freed("azb", "aZ")

        def _recip_rz(e, b=zbk):
            with nc.allow_low_precision(reason="1/Z rounded once to bf16"):
                return e.reciprocal(out=rb[0:1, :], in_=bank(b)[0:1, :])
        p.emit("vector", _recip_rz,
               wait=p.R(("bk", zbk)) + p.F(("rb", 0)), inc=(("aRZ", 1),))
        p.emit("tensor", (lambda e, b=zbk: e.matmul(
            bank(b), onesrb[0:1, :], rb[0:1, :], start=True, stop=True)),
            wait=[("aRZ", p.v("aRZ"))], inc=(("aBC", 1),))
        p.freed(("rb", 0), "aBC")
        p.ready(("bk", zbk), "aBC")
        p.emit("scalar", (lambda e, b=zbk, sl=pr % 2: e.copy(
            out=rzbb[sl][:, :], in_=bank(b))),
            wait=p.R(("bk", zbk)) + p.F(("rzbb", pr % 2)), inc=(("aBB", 1),))
        p.freed(("bk", zbk), "aBB")
        p.emit("vector", (lambda e, b=pvb, sl=pr % 2, qo=qoff: e.tensor_tensor(
            out=abuf[:, qo:qo + 512], in0=bank(b), in1=rzbb[sl][:, :],
            op=ALU.mult)),
            wait=[("aBB", p.v("aBB"))] + p.R(("bk", pvb)), inc=(("aAH", 1),))
        p.freed(("bk", pvb), "aAH")
        p.freed(("rzbb", pr % 2), "aAH")

    def emit_scale_finalize():
        w = [("w1abs", 256), ("w2abs", 256)]
        for which, hb, hn in [(1, hs1b, hs1n), (2, hs2b, hs2n)]:
            wp = w1p if which == 1 else w2p
            p.emit("vector", (lambda e, t=wp: e.tensor_reduce(
                out=wtot[:, 0:1], in_=t[:, :], axis=AX.X, op=ALU.add)),
                wait=w + p.F("wtot"), inc=(("hsv", 1),))
            p.emit("tensor", (lambda e: e.matmul(
                bank(0)[0:1, 0:1], onescf[:, 0:1], wtot[:, 0:1],
                start=True, stop=True)),
                wait=[("hsv", p.v("hsv"))] + p.F(("bk", 0)), inc=(("hsm", 1),))
            p.ready(("bk", 0), "hsm")
            p.emit("scalar", (lambda e: e.activation(
                out=hst[0:1, 0:1], in_=bank(0)[0:1, 0:1], func=AF.Identity,
                scale=0.5 / (FF * C))),
                wait=[("hsm", p.v("hsm"))], inc=(("hss", 1),))
            p.freed(("bk", 0), "hss")
            p.freed("wtot", "hss")
            p.emit("tensor", (lambda e: e.matmul(
                bank(1)[:, 0:1], onesrf[0:1, :], hst[0:1, 0:1],
                start=True, stop=True)),
                wait=[("hss", p.v("hss"))] + p.F(("bk", 1)), inc=(("hsm", 1),))
            p.ready(("bk", 1), "hsm")
            p.emit("scalar", (lambda e, t=hb: e.copy(out=t[:, :], in_=bank(1)[:, 0:1])),
                   wait=[("hsm", p.v("hsm"))], inc=(("hsx", 1),))
            p.freed(("bk", 1), "hsx")
            p.emit("vector", (lambda e, ti=hn, t=hb: e.tensor_scalar_mul(
                out=ti[:, :], in0=t[:, :], scalar1=-1.0)),
                wait=[("hsx", p.v("hsx"))], inc=(("hsx", 1),))
        p.ready("hs", "hsx")

    q1sl = [(w1[128 * r:128 * (r + 1), :], w1qd[128 * r:128 * (r + 1), :])
            for r in range(64)]
    q2sl = [(w2[128 * r:128 * (r + 1), 2048 * fb:2048 * (fb + 1)],
             w2qd[128 * r:128 * (r + 1), 2048 * fb:2048 * (fb + 1)])
            for r in range(16) for fb in range(4)]
    qn = {"i": 0, "ph": 0}

    def emit_quant_ops(k):
        for _ in range(k):
            i = qn["i"]
            if i >= 128:
                return
            ph = qn["ph"]
            src, dst = q1sl[i] if i < 64 else q2sl[i - 64]
            hsb = hs1b if i < 64 else hs2b
            hsn = hs1n if i < 64 else hs2n
            wsem = "w1qw" if i < 64 else "w2qw"
            slot = i % 2
            if ph == 0:
                p.emit("scalar", (lambda e, d=qwf[slot], sr=src: e.dma_start(
                    out=d[:, :], in_=sr)),
                    wait=p.R("hs") + p.F(("qwf", slot)), inc=((f"qwl{slot}", 16),))
                p.ready(("qwfd", slot), f"qwl{slot}")
                p.emit("vector", (lambda e, sl=slot, hn=hsn: e.tensor_single_scalar(
                    out=qc2[:, 0:1024], in_=qwf[sl][:, 0:1024],
                    scalar=hn[:, 0:1], op=ALU.is_lt)),
                    wait=p.R(("qwfd", slot), "hs") + p.F(("qqt", slot)))
            elif ph == 1:
                p.emit("vector", (lambda e, sl=slot, hb=hsb: e.scalar_tensor_tensor(
                    out=qqt[sl][:, 0:1024], in0=qwf[sl][:, 0:1024],
                    scalar=hb[:, 0:1], in1=qc2[:, 0:1024],
                    op0=ALU.is_gt, op1=ALU.subtract)))
            elif ph == 2:
                p.emit("vector", (lambda e, sl=slot, hn=hsn: e.tensor_single_scalar(
                    out=qc2[:, 1024:2048], in_=qwf[sl][:, 1024:2048],
                    scalar=hn[:, 0:1], op=ALU.is_lt)))
            else:
                p.emit("vector", (lambda e, sl=slot, hb=hsb: e.scalar_tensor_tensor(
                    out=qqt[sl][:, 1024:2048], in0=qwf[sl][:, 1024:2048],
                    scalar=hb[:, 0:1], in1=qc2[:, 1024:2048],
                    op0=ALU.is_gt, op1=ALU.subtract)),
                    inc=(("qst", 1),))
                p.freed(("qwf", slot), "qst")
                p.emit("scalar", (lambda e, sl=slot, d=dst: e.dma_start(
                    out=d, in_=qqt[sl][:, :])),
                    wait=[("qst", p.v("qst"))], inc=((wsem, 16),))
                p.freed(("qqt", slot), wsem)
            qn["ph"] = (ph + 1) % 4
            if qn["ph"] == 0:
                qn["i"] += 1

    for pr in range(2 * HQ):
        emit_pair(pr)
        if pr == 5:
            emit_scale(520)
            assert sc["i1"] == 256 and sc["i2"] == 256, sc
            emit_scale_finalize()
        if pr >= 6:
            emit_quant_ops(4)
    emit_quant_ops(4 * 128)
    assert qn["i"] == 128, qn

    AH_ALL = [("aAH", p.v("aAH"))]
    for hd in range(CK):
        slot = hd % 2
        p.emit("scalar", (lambda e, d=wor[slot],
                          s=wo[128 * hd:128 * (hd + 1), :]: e.dma_start(out=d[:, :], in_=s)),
               wait=[("qst", 128)] + p.F(("wor", slot)), inc=((f"wol{slot}", 16),))
        p.ready(("word", slot), f"wol{slot}")
        p.emit("vector", (lambda e, s=slot, c=hd: e.tensor_copy(
            out=wob[:, 2048 * c:2048 * (c + 1)], in_=wor[s][:, :])),
            wait=p.R(("word", slot)) + AH_ALL, inc=(("woc", 1),))
        p.freed(("wor", slot), "woc")
        p.ready(("wobd", hd), "woc")
    for pw in range(CK * 2):
        ck, jt = pw // 2, pw % 2
        bk = pw % 2
        for hd in range(CK):
            w = p.R(("wobd", hd))
            if hd == 0:
                w += AH_ALL + p.F(("bk", bk))
            p.emit("tensor", (lambda e, b=bk, d=hd, c=ck, j=jt: e.matmul(
                bank(b), wob[:, 2048 * d + 128 * c:2048 * d + 128 * (c + 1)],
                abuf[:, 1024 * d + 512 * j:1024 * d + 512 * (j + 1)],
                start=(d == 0), stop=(d == CK - 1))),
                wait=w, inc=((("smm", 1),) if hd == CK - 1 else ()))
        p.ready(("bk", bk), "smm")
        p.emit("scalar", (lambda e, b=bk, s=pw % 2: e.copy(
            out=stv[s][:, :], in_=bank(b))),
            wait=p.R(("bk", bk)) + p.F(("stv", pw % 2)), inc=(("sevc", 1),))
        p.freed(("bk", bk), "sevc")
        p.emit("scalar", (lambda e, s=pw % 2,
                          d=ssc[128 * ck:128 * (ck + 1), 512 * jt:512 * (jt + 1)]:
                          e.dma_start(out=d, in_=stv[s][:, :])),
               wait=[("sevc", p.v("sevc"))], inc=(("sev", 16),))
        p.freed(("stv", pw % 2), "sev")
    SMM_END = p.v("smm")
    SEV_ALL = [("sev", p.v("sev"))]

    for ck in range(CK):
        p.emit("sync", (lambda e, c=ck, s=xnsc[0:OWN, 128 * ck:128 * (ck + 1)]:
                        e.dma_start_transpose(xnT[:, 1024 * c:1024 * (c + 1)], s)),
               wait=[("smm", SMM_END)], inc=(("xta", 16),))
    p.ready("xnTall", "xta")

    W1QW = [("w1qw", p.v("w1qw"))]
    W2QW = [("w2qw", p.v("w2qw"))]

    for fq in range(4):
        for qd in range(4):
            gqd = fq * 4 + qd
            slot = gqd % 2
            for ck in range(CK):
                w = p.F(("m1r", slot)) if ck == 0 else []
                if gqd == 0 and ck == 0:
                    w = w + W1QW + [("qst", 128), ("woc", 16), ("aPV", 1024)]
                if gqd == 1 and ck == 0:
                    w = w + [("qst", 128), ("aPV", 1024)] + W2QW + SEV_ALL
                p.emit("sync", (lambda e, s=slot, c=ck,
                                sr=w1qd[512 * gqd:512 * (gqd + 1),
                                        128 * ck:128 * (ck + 1)]:
                                e.dma_start_transpose(
                                    m1r[s][:, 512 * c:512 * (c + 1)], sr)),
                       wait=w, inc=((f"m1l{slot}", 16),))
            p.ready(("m1rd", slot), f"m1l{slot}")
            for fi in range(4):
                fg = gqd * 4 + fi
                fiq = qd * 4 + fi
                for jt in range(2):
                    bk = (2 * fiq + jt) % 4
                    for ck in range(CK):
                        w = []
                        if ck == 0:
                            w = p.R("xnTall", ("m1rd", slot)) + p.F(("bk", bk))
                        p.emit("tensor", (lambda e, b=bk, s=slot, c=ck, f=fi, j=jt:
                                          e.matmul(bank(b),
                                                   m1r[s][:, 512 * c + 128 * f:512 * c + 128 * (f + 1)],
                                                   xnT[:, 1024 * c + 512 * j:1024 * c + 512 * (j + 1)],
                                                   start=(c == 0), stop=(c == CK - 1))),
                               wait=w, inc=((("m1m", 1),) if ck == CK - 1 else ()))
                    p.ready(("bk", bk), "m1m")
                    p.emit("scalar", (lambda e, b=bk, fi_=fiq, j=jt, f=fg:
                                      e.activation(
                        out=hbuf[:, 1024 * fi_ + 512 * j:1024 * fi_ + 512 * (j + 1)],
                        in_=bank(b), func=AF.Gelu, bias=b1s[:, f:f + 1])),
                        wait=p.R(("bk", bk)) + CSTW, inc=(("gel", 1),))
                    p.freed(("bk", bk), "gel")
            p.rel[("m1r", slot)] = [("m1m", p.v("m1m"))]
        GEL = [("gel", p.v("gel"))]
        for cq in range(4):
            gcq = fq * 4 + cq
            for oct_ in range(2):
                goct = gcq * 2 + oct_
                slot = goct % 3
                for fc in range(8):
                    fcq = oct_ * 8 + fc
                    fgl = fq * 16 + fcq
                    w = p.F(("m2r", slot)) if fc == 0 else []
                    if goct == 0 and fc == 0:
                        w = w + W2QW + [("smm", 32), ("aAH", 32)]
                    if goct == 1 and fc == 0:
                        w = w + [("aAH", 32)]
                    p.emit("sync", (lambda e, s=slot, f=fc,
                                    sr=w2qd[512 * cq:512 * (cq + 1),
                                            128 * fgl:128 * (fgl + 1)]:
                                    e.dma_start_transpose(
                                        m2r[s][:, 512 * f:512 * (f + 1)], sr)),
                           wait=w, inc=((f"m2l{slot}", 16),))
                p.ready(("m2rd", slot), f"m2l{slot}")
                for pr2 in range(8):
                    ckq, jt = pr2 // 2, pr2 % 2
                    bk = pr2
                    for fc in range(8):
                        fcq = oct_ * 8 + fc
                        w = []
                        if fc == 0:
                            w += p.R(("m2rd", slot)) + GEL
                            if oct_ == 0:
                                w += p.F(("bk", bk))
                        p.emit("tensor", (lambda e, b=bk, s=slot, f=fc, fi_=fcq,
                                          j=jt, c=ckq, oc=oct_:
                                          e.matmul(bank(b),
                                                   m2r[s][:, 512 * f + 128 * c:512 * f + 128 * (c + 1)],
                                                   hbuf[:, 1024 * fi_ + 512 * j:1024 * fi_ + 512 * (j + 1)],
                                                   start=(oc == 0 and f == 0),
                                                   stop=(oc == 1 and f == 7))),
                               wait=w,
                               inc=((("m2m", 1),) if (oct_ == 1 and fc == 7) else ()))
                    if oct_ == 1:
                        p.ready(("bk", bk), "m2m")
                        ckg = cq * 4 + ckq
                        acc = accl if ckg < 8 else acch
                        ao = (ckg % 8) * 1024 + 512 * jt
                        if fq == 0:
                            p.emit("vector", (lambda e, b=bk, a=acc, o=ao:
                                              e.tensor_copy(out=a[:, o:o + 512],
                                                            in_=bank(b))),
                                   wait=p.R(("bk", bk)), inc=(("m2e", 1),))
                        else:
                            p.emit("vector", (lambda e, b=bk, a=acc, o=ao:
                                              e.tensor_add(out=a[:, o:o + 512],
                                                           in0=a[:, o:o + 512],
                                                           in1=bank(b))),
                                   wait=p.R(("bk", bk)), inc=(("m2e", 1),))
                        p.freed(("bk", bk), "m2e")
                p.rel[("m2r", slot)] = [("m2m", p.v("m2m"))] if oct_ == 1 else \
                    [("m2m", p.v("m2m") + (8 if oct_ == 0 else 0))]
        for key in [("hbuf", 0)]:
            p.rel[key] = [("m2m", p.v("m2m"))]
    M2E = [("m2e", p.v("m2e"))]

    for ck in range(CK):
        acc = accl if ck < 8 else acch
        ao = (ck % 8) * 1024
        slot = ck % 2
        p.emit("scalar", (lambda e, s=slot, sr=ssc[128 * ck:128 * (ck + 1), :]:
                          e.dma_start(out=stre[s][:, :], in_=sr)),
               wait=SEV_ALL + M2E + p.F(("stre", slot)), inc=((f"stl{slot}", 16),))
        p.ready(("strd", slot), f"stl{slot}")
        p.emit("scalar", (lambda e, a=acc, o=ao, c=ck: e.activation(
            out=gtmp[:, :], in_=a[:, o:o + 1024], func=AF.Silu,
            bias=b2s[:, c:c + 1])),
            wait=M2E + p.F("gtmp"), inc=(("sil", 1),))
        p.emit("vector", (lambda e, a=acc, o=ao, s=slot: e.tensor_tensor(
            out=a[:, o:o + 1024], in0=gtmp[:, :], in1=stre[s][:, :],
            op=ALU.mult)),
            wait=[("sil", p.v("sil"))] + p.R(("strd", slot)),
            inc=(("sgv", 1),))
        p.freed("gtmp", "sgv")
        p.freed(("stre", slot), "sgv")
    SGV = [("sgv", p.v("sgv"))]
    for tc in range(8):
        slot = tc % 2
        pst = psA if slot == 0 else psB
        pkeys = [("bk", 0 + 4 * slot), ("bk", 1 + 4 * slot),
                 ("bk", 2 + 4 * slot), ("bk", 3 + 4 * slot)]
        p.emit("sync", (lambda e, s=slot, sr=xb[128 * tc:128 * (tc + 1), :]:
                        e.dma_start(out=xoring[:, 2048 * s:2048 * (s + 1)], in_=sr)),
               wait=SGV + p.F(("xo", slot)), inc=((f"xo{slot}", 16),))
        p.ready(("xod", slot), f"xo{slot}")
        for ck in range(CK):
            acc = accl if ck < 8 else acch
            ao = (ck % 8) * 1024
            w = SGV + CSTW
            if ck == 0:
                w += p.F(*pkeys)
            p.emit("tensor", (lambda e, pp=pst, a=acc, o=ao, t=tc, c=ck:
                              e.transpose(pp[:, 128 * c:128 * (c + 1)],
                                          a[:, o + 128 * t:o + 128 * (t + 1)],
                                          identf[:, :])),
                   wait=w if ck == 0 else (),
                   inc=((("otp", 1),) if ck == CK - 1 else ()))
        p.emit("vector", (lambda e, pp=pst, s=slot: e.tensor_add(
            out=xoring[:, 2048 * s:2048 * (s + 1)],
            in0=xoring[:, 2048 * s:2048 * (s + 1)], in1=pp[:, :])),
            wait=[("otp", p.v("otp"))] + p.R(("xod", slot)),
            inc=(("oadd", 1),))
        for k in pkeys:
            p.freed(k, "oadd")
        p.emit("sync", (lambda e, s=slot, d=out_d[128 * tc:128 * (tc + 1), :]:
                        e.dma_start(out=d, in_=xoring[:, 2048 * s:2048 * (s + 1)])),
               wait=[("oadd", p.v("oadd"))], inc=(("outw", 16),))
        p.freed(("xo", slot), "outw")
    p.emit("sync", None, wait=[("outw", 8 * 16)])

    sems = {}
    for name in p.sems_used():
        sems[name] = ctx.enter_context(nc.semaphore(name))

    def replay(engine_obj, key):
        for fn, waits, incs in p.ops[key]:
            for s, v in waits:
                engine_obj.wait_ge(sems[s], v)
            if fn is not None:
                ins = fn(engine_obj)
                for s, n in incs:
                    r = ins.then_inc(sems[s], n)
                    ins = r if r is not None else ins

    with nc.Block() as block:
        @block.sync
        def _(sync):
            replay(sync, "sync")

        @block.scalar
        def _(scalar):
            replay(scalar, "scalar")

        @block.vector
        def _(vector):
            replay(vector, "vector")

        @block.tensor
        def _(tensor):
            replay(tensor, "tensor")

        @block.gpsimd
        def _(gpsimd):
            replay(gpsimd, "gpsimd")

    ctx.close()
    return nc


_NC_CACHE = {}


def kernel(L0_attn_x_q, L0_attn_Wq, L0_attn_Wk, L0_attn_Wv, L0_attn_Wo,
           L0_ffn_w1, L0_ffn_b1, L0_ffn_w2, L0_ffn_b2):
    x = np.ascontiguousarray(np.asarray(L0_attn_x_q, dtype=np.float32))
    wqa = np.ascontiguousarray(np.asarray(L0_attn_Wq, dtype=np.float32).reshape(C, HQ * D))
    wka = np.ascontiguousarray(np.asarray(L0_attn_Wk, dtype=np.float32).reshape(C, HKV * D))
    wva = np.ascontiguousarray(np.asarray(L0_attn_Wv, dtype=np.float32).reshape(C, HKV * D))
    woa = np.ascontiguousarray(np.asarray(L0_attn_Wo, dtype=np.float32))
    w1a = np.ascontiguousarray(np.asarray(L0_ffn_w1, dtype=np.float32))
    w2a = np.ascontiguousarray(np.asarray(L0_ffn_w2, dtype=np.float32))
    b1cv = np.ascontiguousarray(np.asarray(L0_ffn_b1, dtype=np.float32).reshape(FF // 128, 128).T)
    b2cv = np.ascontiguousarray(np.asarray(L0_ffn_b2, dtype=np.float32).reshape(C // 128, 128).T)
    identf = np.eye(128, dtype=np.float32)
    onescb = np.ones((128, 1), dtype=ml_dtypes.bfloat16)
    onesrb = np.ones((1, 128), dtype=ml_dtypes.bfloat16)
    onescf = np.ones((128, 1), dtype=np.float32)
    onesrf = np.ones((1, 128), dtype=np.float32)
    epsc = np.zeros((128, 4), dtype=np.float32)
    epsc[:, 0] = np.float32(EPS_LN)
    epsc[:, 1] = np.float32(EPS_QK)
    epsc[:, 2] = np.float32(EPS_QK_128)

    if "nc" not in _NC_CACHE:
        _NC_CACHE["nc"] = build_nc()
    nc = _NC_CACHE["nc"]

    in_maps = []
    for core in range(8):
        b, qt = core // 4, core % 4
        xrot = np.ascontiguousarray(np.roll(x[b], -qt * OWN, axis=0))
        in_maps.append({
            "xb": xrot, "wq": wqa, "wk": wka, "wv": wva, "wo": woa,
            "w1": w1a, "w2": w2a, "b1c": b1cv, "b2c": b2cv,
            "identf": identf, "onescb": onescb, "onesrb": onesrb,
            "onescf": onescf, "onesrf": onesrf, "epsc": epsc,
        })
    res = run_bass_kernel_spmd(nc, in_maps, core_ids=list(range(8)))
    out = np.empty((B, T, C), dtype=np.float32)
    for core in range(8):
        b, qt = core // 4, core % 4
        out[b, qt * OWN:(qt + 1) * OWN, :] = res.results[core]["out"]
    return out
